# version 14
# baseline (speedup 1.0000x reference)
import contextlib

import numpy as np
import ml_dtypes

import concourse.bass as bass
import concourse.mybir as mybir
from concourse.bass_utils import run_bass_kernel_spmd

F32 = mybir.dt.float32
BF16 = mybir.dt.bfloat16
AF = mybir.ActivationFunctionType
ALU = mybir.AluOpType
AX = mybir.AxisListType

NCORES = 8
TOWN = 1024
HALO = 512
TPAD = TOWN + HALO
D = 4096
DC = 32
NQ = 32
NKV = 8
GQA = 4
HD = 128
WIN = 640
NQT = 8
OC = 256
NOC = D // OC
NE = 28
RK = 8
ER = NE * RK
SCALE = 1.0 / float(np.sqrt(128.0))
EPS = 1e-5
ARENA = 211968

ENGS = ("sync", "gpsimd", "tensor", "vector", "scalar")


class Plan:

    def __init__(self, nc, ctx):
        self.nc = nc
        self.ops = {e: [] for e in ENGS}
        self.sems = {}
        self.cnt = {}
        self.hiw = {e: {} for e in ENGS}
        for name in ("ldx", "ldw", "st", "pe", "dve", "act", "pool"):
            self.sems[name] = ctx.enter_context(nc.semaphore(name))
            self.cnt[name] = 0

    def do(self, eng, fn, inc=None, dinc=None, waits=()):
        sname, amount = None, 0
        if inc is not None:
            sname, amount = inc, 1
        if dinc is not None:
            sname, amount = dinc, 16
        eff = []
        hw = self.hiw[eng]
        for (n, v) in waits:
            if v > hw.get(n, 0):
                hw[n] = v
                eff.append((n, v))
        if sname is not None:
            self.cnt[sname] += amount
        ret = self.cnt[sname] if sname is not None else None
        sems = self.sems

        def closure(e, fn=fn, eff=tuple(eff), sname=sname, amount=amount):
            for (n, v) in eff:
                e.wait_ge(sems[n], v)
            if fn is not None:
                inst = fn(e)
                if sname is not None:
                    inst.then_inc(sems[sname], amount)

        self.ops[eng].append(closure)
        return ret

    def barrier(self):
        snap = dict(self.cnt)
        for e in ENGS:
            self.do(e, None, waits=list(snap.items()))

    def play(self, block):
        for e in ENGS:
            lst = self.ops[e]
            if lst:
                getattr(block, e)(lambda eng, lst=lst: [f(eng) for f in lst])


DEBUG_XNT = False


def build_nc():
    nc = bass.Bass("TRN2", target_bir_lowering=False)

    xpad = nc.dram_tensor("xpad", [TPAD, D], F32, kind="ExternalInput").ap()
    wqT = nc.dram_tensor("wqT", [D, NQ * HD], F32, kind="ExternalInput").ap()
    wkT = nc.dram_tensor("wkT", [D, NKV * HD], F32, kind="ExternalInput").ap()
    wvT = nc.dram_tensor("wvT", [D, NKV * HD], F32, kind="ExternalInput").ap()
    woT = nc.dram_tensor("woT", [D, D], F32, kind="ExternalInput").ap()
    baseT = nc.dram_tensor("baseT", [D, D], F32, kind="ExternalInput").ap()
    gwT = nc.dram_tensor("gwT", [D, NE], F32, kind="ExternalInput").ap()
    laT = nc.dram_tensor("laT", [D, ER], F32, kind="ExternalInput").ap()
    lbT = nc.dram_tensor("lbT", [ER, D], F32, kind="ExternalInput").ap()
    m01 = nc.dram_tensor("m01", [5, 128, WIN], BF16, kind="ExternalInput").ap()
    ident_i = nc.dram_tensor("ident", [128, 128], BF16, kind="ExternalInput").ap()
    out = nc.dram_tensor("out", [TOWN, D], F32, kind="ExternalOutput").ap()
    if DEBUG_XNT:
        dbg_xnT = nc.dram_tensor("dbg_xnT", [128, DC * TOWN], F32, kind="ExternalOutput").ap()
        dbg_halo = nc.dram_tensor("dbg_halo", [128, DC * HALO], F32, kind="ExternalOutput").ap()

    ctx = contextlib.ExitStack()
    arena = ctx.enter_context(nc.sbuf_tensor("arena", [128, ARENA], mybir.dt.uint8))
    arena_addr = nc.lookup_mloc(arena).addr

    def at(name, shape, dtype, off):
        h = nc.virtual_sbuf_tensor(name, shape, dtype)
        nc._tensor(h.name, h.shape, h.dtype, type="SB")
        mloc = nc.lookup_mloc(h)
        mloc.allocated = not nc.target_bir_lowering
        mloc.addr = arena_addr + off
        return h

    xnT = at("xnT", [128, DC * TOWN], BF16, 0)
    ident = at("ident_sb", [128, 128], BF16, 65536)
    st = at("stats", [128, 256], F32, 65792)
    Ha = at("Ha", [112, TOWN], BF16, 66816)
    Hb = at("Hb", [112, TOWN], BF16, 68864)
    O = at("O", [128, DC * TOWN], BF16, 70912)
    haloT = at("haloT", [128, DC * HALO], BF16, 70912)
    xstg = [at("xstg0", [128, D], F32, 103680),
            at("xstg1", [128, D], F32, 120064),
            at("xstg2", [128, D], F32, 152832)]
    wvstr = [at("wvstr0", [128, DC * 256], BF16, 103680),
             at("wvstr1", [128, DC * 256], BF16, 120064)]
    K = at("K", [128, NKV * TPAD], BF16, 136448)
    V = at("V", [128, 12 * TOWN], BF16, 161024)
    sqbuf = at("sqbuf", [128, D], F32, 136448)
    xn_tm = [at("xn_tm0", [128, D], BF16, 185600),
             at("xn_tm1", [128, D], BF16, 193792)]
    wkstr = [at("wkstr0", [128, DC * 128], BF16, 185600),
             at("wkstr1", [128, DC * 128], BF16, 193792)]
    ATstr = at("ATstr", [128, DC * ER], BF16, 185600)
    gwstr = at("gwstr", [128, DC * NE], BF16, 199936)
    Hf32 = at("Hf32", [128, ER], F32, 201728)
    Hptm = at("Hptm", [128, ER], BF16, 202624)
    gbuf = at("gbuf", [128, NE], F32, 203072)
    eq1b = at("eq1b", [128, NE], F32, 203200)
    eq2b = at("eq2b", [128, NE], F32, 203328)
    mkb = at("mkb", [128, NE], F32, 203456)
    wqstr = at("wqstr", [128, DC * 128], BF16, 185600)
    Qsb = [at("Qsb0", [128, TOWN], BF16, 193792),
           at("Qsb1", [128, TOWN], BF16, 195840)]
    praw = [at("praw0", [128, WIN], BF16, 197888),
            at("praw1", [128, WIN], BF16, 199168)]
    pm = [at("pm0", [128, WIN], BF16, 200448),
          at("pm1", [128, WIN], BF16, 201728)]
    PTsb = [at("PTsb0", [128, WIN], BF16, 203008),
            at("PTsb1", [128, WIN], BF16, 204288)]
    maskb = at("maskb", [128, 5 * WIN], BF16, 205568)
    bstr = [at("bstr0", [128, DC * OC], BF16, 136448),
            at("bstr1", [128, DC * OC], BF16, 152832)]
    wstr = [at("wstr0", [128, DC * OC], BF16, 169216),
            at("wstr1", [128, DC * OC], BF16, 185600)]
    xc2 = [at("xc20", [128, 2 * OC], F32, 201984),
           at("xc21", [128, 2 * OC], F32, 204032)]
    BTall = at("BTall", [112, 2 * OC], BF16, 206080)
    outst = [at("outst0", [128, OC], F32, 208128), at("outst1", [128, OC], F32, 209152)]

    xnT3 = xnT.ap().rearrange("p (c n) -> p c n", c=DC)
    haloT3 = haloT.ap().rearrange("p (c n) -> p c n", c=DC)
    K3 = K.ap().rearrange("p (h n) -> p h n", h=NKV)
    V3 = V.ap().rearrange("p (t n) -> p t n", t=12)
    O3 = O.ap().rearrange("p (c n) -> p c n", c=DC)
    ATstr3 = ATstr.ap().rearrange("p (c n) -> p c n", c=DC)
    gwstr3 = gwstr.ap().rearrange("p (c n) -> p c n", c=DC)
    wqstr3 = wqstr.ap().rearrange("p (c n) -> p c n", c=DC)
    wvstr3 = [w.ap().rearrange("p (c n) -> p c n", c=DC) for w in wvstr]
    wkstr3 = [w.ap().rearrange("p (c n) -> p c n", c=DC) for w in wkstr]
    bstr3 = [w.ap().rearrange("p (c n) -> p c n", c=DC) for w in bstr]
    wstr3 = [w.ap().rearrange("p (c n) -> p c n", c=DC) for w in wstr]

    BF = [ctx.enter_context(nc.psum_tensor(f"psF{i}", [128, 512], F32)) for i in range(6)]
    Bb = [ctx.enter_context(nc.psum_tensor(f"psB{i}", [128, 1024], BF16)) for i in range(2)]
    BB = Bb[0]

    p = Plan(nc, ctx)

    C_EPS = 0
    C_A = lambda tt: 8 + (tt % 2) * 8
    C_M1, C_M2, C_DM, C_P1, C_P2 = 24, 25, 26, 27, 28
    C_B = lambda par: 32 + par * 8
    C_SUM = lambda par: 48 + par
    C_RCP = lambda par: 50 + par

    stap = st.ap()

    p.do("vector", lambda e: e.memset(stap[:, C_EPS:C_EPS + 1], EPS), inc="dve")
    ld_id = p.do("sync", lambda e: e.dma_start(out=ident.ap()[:], in_=ident_i[:]), dinc="ldx")

    NTT = TPAD // 128
    xn_done = {}
    var_done = {}
    trevac = {}
    ldx_t = {}
    sq_done = {}
    sum_done = {}

    def a_load(tt):
        w = [("dve", xn_done.get(tt - 3, 0))]
        ldx_t[tt] = p.do("sync",
                         lambda e, tt=tt: e.dma_start(
                             out=xstg[tt % 3].ap()[:],
                             in_=xpad[tt * 128:(tt + 1) * 128, :]),
                         dinc="ldx", waits=w)

    def a_stats1(tt):
        ca = C_A(tt)
        sq_done[tt] = p.do("scalar",
                           lambda e, tt=tt, ca=ca: e.activation(
                               sqbuf.ap()[:], xstg[tt % 3].ap()[:], AF.Square,
                               accum_out=stap[:, ca + 1:ca + 2]),
                           inc="act",
                           waits=[("ldx", ldx_t[tt]), ("dve", var_done.get(tt - 2, 0))])
        sum_done[tt] = p.do("vector",
                            lambda e, tt=tt, ca=ca: e.reduce_sum(
                                stap[:, ca:ca + 1], xstg[tt % 3].ap()[:], axis=AX.X),
                            inc="dve", waits=[("ldx", ldx_t[tt])])

    def a_stats2(tt):
        ca = C_A(tt)
        d = p.do("vector",
                 lambda e, ca=ca: e.tensor_scalar(
                     stap[:, ca + 2:ca + 3], stap[:, ca:ca + 1], -1.0 / D, None,
                     op0=ALU.mult),
                 inc="dve", waits=[("dve", sum_done[tt])])
        d = p.do("vector",
                 lambda e, ca=ca: e.tensor_scalar(
                     stap[:, ca + 3:ca + 4], stap[:, ca + 2:ca + 3],
                     stap[:, ca + 2:ca + 3], None, op0=ALU.mult),
                 inc="dve", waits=[("dve", d)])
        v_d = p.do("vector",
                   lambda e, ca=ca: e.scalar_tensor_tensor(
                       stap[:, ca + 4:ca + 5], stap[:, ca + 1:ca + 2], 1.0 / D,
                       stap[:, ca + 3:ca + 4], op0=ALU.mult, op1=ALU.subtract),
                   inc="dve", waits=[("act", sq_done[tt]), ("dve", d)])
        var_done[tt] = v_d
        d = p.do("scalar",
                 lambda e, ca=ca: e.activation(
                     stap[:, ca + 5:ca + 6], stap[:, ca + 4:ca + 5], AF.Ln,
                     bias=stap[:, C_EPS:C_EPS + 1]),
                 inc="act", waits=[("dve", v_d)])
        rs_d = p.do("scalar",
                    lambda e, ca=ca: e.activation(
                        stap[:, ca + 6:ca + 7], stap[:, ca + 5:ca + 6], AF.Exp,
                        scale=-0.5),
                    inc="act", waits=[("act", d)])
        xn_done[tt] = p.do("vector",
                           lambda e, tt=tt, ca=ca: e.tensor_scalar(
                               xn_tm[tt % 2].ap()[:], xstg[tt % 3].ap()[:],
                               stap[:, ca + 2:ca + 3], stap[:, ca + 6:ca + 7],
                               op0=ALU.add, op1=ALU.mult),
                           inc="dve",
                           waits=[("act", rs_d), ("pe", trans_done.get(tt - 2, 0))])

    trans_done = {}

    def a_trans(tt):
        for g in range(8):
            gi = tt * 8 + g
            prev_ev = trevac.get(gi - 2, 0)
            esem = "act" if g % 2 == 0 else "dve"
            tr_d = None
            for k in range(4):
                c = 4 * g + k
                tr_d = p.do("tensor",
                            lambda e, c=c, k=k, g=g, tt=tt: e.transpose(
                                Bb[g % 2].ap()[:, k * 128:(k + 1) * 128],
                                xn_tm[tt % 2].ap()[:, c * 128:(c + 1) * 128],
                                ident.ap()[:]),
                            inc="pe",
                            waits=[("dve", xn_done[tt]), (esem, prev_ev),
                                   ("ldx", ld_id)])
            if tt < 4:
                dst = haloT3[:, 4 * g:4 * g + 4, tt * 128:(tt + 1) * 128]
            else:
                dst = xnT3[:, 4 * g:4 * g + 4, (tt - 4) * 128:(tt - 3) * 128]
            if g % 2 == 0:
                ev = p.do("scalar",
                          lambda e, dst=dst, g=g: e.copy(
                              dst, Bb[g % 2].ap()[:, 0:512].rearrange("p (k n) -> p k n", k=4)),
                          inc="act", waits=[("pe", tr_d)])
            else:
                ev = p.do("vector",
                          lambda e, dst=dst, g=g: e.tensor_copy(
                              dst, Bb[g % 2].ap()[:, 0:512].rearrange("p (k n) -> p k n", k=4)),
                          inc="dve", waits=[("pe", tr_d)])
            trevac[gi] = ev
        trans_done[tt] = p.cnt["pe"]

    a_load(0)
    a_load(1)
    for tt in range(NTT + 2):
        if tt < NTT:
            a_stats1(tt)
        if 0 <= tt - 1 < NTT:
            a_stats2(tt - 1)
        if tt + 2 < NTT:
            a_load(tt + 2)
        if 0 <= tt - 2 < NTT:
            a_trans(tt - 2)

    p.barrier()

    if DEBUG_XNT:
        dzc = {}
        for half in range(16):
            dzc[half] = p.do("vector", lambda e, half=half: e.tensor_copy(
                xstg[0].ap()[:, 0:2048].rearrange("p (a b) -> p a b", a=1),
                xnT.ap()[:, half * 2048:(half + 1) * 2048].rearrange("p (a b) -> p a b", a=1)),
                inc="dve")
            p.do("sync", lambda e, half=half: e.dma_start(
                out=dbg_xnT[:, half * 2048:(half + 1) * 2048], in_=xstg[0].ap()[:, 0:2048]),
                dinc="st", waits=[("dve", dzc[half])])
            p.barrier()
        for half in range(8):
            dzc[half] = p.do("vector", lambda e, half=half: e.tensor_copy(
                xstg[0].ap()[:, 0:2048],
                haloT.ap()[:, half * 2048:(half + 1) * 2048]),
                inc="dve")
            p.do("sync", lambda e, half=half: e.dma_start(
                out=dbg_halo[:, half * 2048:(half + 1) * 2048], in_=xstg[0].ap()[:, 0:2048]),
                dinc="st", waits=[("dve", dzc[half])])
            p.barrier()

    def slab_rhs(s, width, c):
        if s == 0:
            return haloT3[:, c, 0:width]
        return xnT3[:, c, (s - 1) * 512:(s - 1) * 512 + width]

    kld = {}
    kmm = {}
    kevac = {}

    def k_load(kd):
        return p.do("gpsimd",
                    lambda e, kd=kd: e.dma_start(
                        out=wkstr3[kd % 2],
                        in_=wkT.rearrange("(c p) n -> p c n", p=128)[:, :, kd * 128:(kd + 1) * 128]),
                    dinc="ldw", waits=[("pe", kmm.get(kd - 2, 0))])

    kld[0] = k_load(0)
    kld[1] = k_load(1)
    for kd in range(NKV):
        for s in range(3):
            ps = BF[(kd * 3 + s) % 2]
            mm = None
            for c in range(DC):
                mm = p.do("tensor",
                          lambda e, kd=kd, s=s, c=c, ps=ps: e.matmul(
                              ps.ap()[:], wkstr3[kd % 2][:, c, :], slab_rhs(s, 512, c),
                              start=(c == 0), stop=(c == DC - 1)),
                          inc="pe",
                          waits=[("ldw", kld[kd]),
                                 ("dve", kevac.get((kd * 3 + s) - 2, 0))])
            kmm[kd] = mm
            kevac[kd * 3 + s] = p.do(
                "vector",
                lambda e, kd=kd, s=s, ps=ps: e.tensor_copy(
                    K3[:, kd, s * 512:(s + 1) * 512], ps.ap()[:]),
                inc="dve", waits=[("pe", mm)])
        if kd + 2 < NKV:
            kld[kd + 2] = k_load(kd + 2)

    at_ld = p.do("gpsimd",
                 lambda e: e.dma_start(out=ATstr3,
                                       in_=laT.rearrange("(c p) n -> p c n", p=128)),
                 dinc="ldw", waits=[("pe", kmm[NKV - 1])])
    gw_ld = p.do("gpsimd",
                 lambda e: e.dma_start(out=gwstr3,
                                       in_=gwT.rearrange("(c p) n -> p c n", p=128)),
                 dinc="ldw")

    vld = {}
    vmm = {}
    vevac = {}

    def v_load(q):
        return p.do("gpsimd",
                    lambda e, q=q: e.dma_start(
                        out=wvstr3[q % 2],
                        in_=wvT.rearrange("(c p) n -> p c n", p=128)[:, :, q * 256:(q + 1) * 256]),
                    dinc="ldw", waits=[("pe", vmm.get(q - 2, 0))])

    vld[0] = v_load(0)
    vld[1] = v_load(1)
    vi = 0
    for q in range(4):
        last_mm_q = None
        for tt in range(12):
            ps = BF[2 + vi % 2]
            if tt < 4:
                lhsT = lambda c, tt=tt: haloT3[:, c, tt * 128:(tt + 1) * 128]
            else:
                lhsT = lambda c, tt=tt: xnT3[:, c, (tt - 4) * 128:(tt - 3) * 128]
            mm = None
            for c in range(DC):
                mm = p.do("tensor",
                          lambda e, c=c, ps=ps, lhsT=lhsT, q=q: e.matmul(
                              ps.ap()[:, 0:256], lhsT(c), wvstr3[q % 2][:, c, :],
                              start=(c == 0), stop=(c == DC - 1)),
                          inc="pe",
                          waits=[("ldw", vld[q]), ("dve", vevac.get(vi - 2, 0))])
            last_mm_q = mm
            vevac[vi] = p.do("vector",
                             lambda e, tt=tt, q=q, ps=ps: e.tensor_copy(
                                 V3[:, tt, q * 256:(q + 1) * 256], ps.ap()[:, 0:256]),
                             inc="dve", waits=[("pe", mm)])
            vi += 1
        vmm[q] = last_mm_q
        if q + 2 < 4:
            vld[q + 2] = v_load(q + 2)

    p.barrier()

    mask_ld = p.do("sync",
                   lambda e: e.dma_start(
                       out=maskb.ap()[:].rearrange("p (t w) -> p t w", t=5),
                       in_=m01.rearrange("t p w -> p t w")),
                   dinc="ldx")
    lg_read = 0
    hevac_prev = 0
    htev = {}
    for qt in range(NQT):
        LG = BF[0]
        HM = BF[1]
        mm = None
        for c in range(DC):
            mm = p.do("tensor",
                      lambda e, c=c, qt=qt, LG=LG: e.matmul(
                          LG.ap()[:, 0:NE], xnT3[:, c, qt * 128:(qt + 1) * 128],
                          gwstr3[:, c, :], start=(c == 0), stop=(c == DC - 1)),
                      inc="pe", waits=[("ldw", gw_ld), ("dve", lg_read)])
        m1_d = p.do("vector", lambda e, LG=LG: e.reduce_max(
            stap[:, C_M1:C_M1 + 1], LG.ap()[:, 0:NE], axis=AX.X),
            inc="dve", waits=[("pe", mm)])
        e1_d = p.do("vector", lambda e, LG=LG: e.tensor_scalar(
            eq1b.ap()[:], LG.ap()[:, 0:NE], stap[:, C_M1:C_M1 + 1], None,
            op0=ALU.is_equal), inc="dve", waits=[("dve", m1_d)])
        mk_d = p.do("vector", lambda e, LG=LG: e.scalar_tensor_tensor(
            mkb.ap()[:], eq1b.ap()[:], -1e30, LG.ap()[:, 0:NE],
            op0=ALU.mult, op1=ALU.add), inc="dve", waits=[("dve", e1_d)])
        lg_read = mk_d
        m2_d = p.do("vector", lambda e: e.reduce_max(
            stap[:, C_M2:C_M2 + 1], mkb.ap()[:], axis=AX.X),
            inc="dve", waits=[("dve", mk_d)])
        e2_d = p.do("vector", lambda e: e.tensor_scalar(
            eq2b.ap()[:], mkb.ap()[:], stap[:, C_M2:C_M2 + 1], None,
            op0=ALU.is_equal), inc="dve", waits=[("dve", m2_d)])
        dm_d = p.do("vector", lambda e: e.tensor_tensor(
            stap[:, C_DM:C_DM + 1], stap[:, C_M2:C_M2 + 1],
            stap[:, C_M1:C_M1 + 1], op=ALU.subtract),
            inc="dve", waits=[("dve", e2_d)])
        p1_d = p.do("scalar", lambda e: e.activation(
            stap[:, C_P1:C_P1 + 1], stap[:, C_DM:C_DM + 1], AF.Sigmoid, scale=-1.0),
            inc="act", waits=[("dve", dm_d)])
        p2_d = p.do("scalar", lambda e: e.activation(
            stap[:, C_P2:C_P2 + 1], stap[:, C_DM:C_DM + 1], AF.Sigmoid),
            inc="act", waits=[("act", p1_d)])
        g1_d = p.do("vector", lambda e: e.tensor_scalar(
            gbuf.ap()[:], eq1b.ap()[:], stap[:, C_P1:C_P1 + 1], 2.0,
            op0=ALU.mult, op1=ALU.mult), inc="dve", waits=[("act", p2_d)])
        g2_d = p.do("vector", lambda e: e.tensor_scalar(
            eq2b.ap()[:], eq2b.ap()[:], stap[:, C_P2:C_P2 + 1], 2.0,
            op0=ALU.mult, op1=ALU.mult), inc="dve", waits=[("dve", g1_d)])
        g_d = p.do("vector", lambda e: e.tensor_add(
            gbuf.ap()[:], gbuf.ap()[:], eq2b.ap()[:]),
            inc="dve", waits=[("dve", g2_d)])

        mm2 = None
        for c in range(DC):
            mm2 = p.do("tensor",
                       lambda e, c=c, qt=qt, HM=HM: e.matmul(
                           HM.ap()[:, 0:ER], xnT3[:, c, qt * 128:(qt + 1) * 128],
                           ATstr3[:, c, :], start=(c == 0), stop=(c == DC - 1)),
                       inc="pe", waits=[("ldw", at_ld), ("act", hevac_prev)])
        hev = p.do("scalar", lambda e, HM=HM: e.copy(Hf32.ap()[:], HM.ap()[:, 0:ER]),
                   inc="act", waits=[("pe", mm2)])
        hevac_prev = hev
        hp_d = None
        for ex in range(NE):
            hp_d = p.do("vector",
                        lambda e, ex=ex: e.tensor_scalar(
                            Hptm.ap()[:, ex * RK:(ex + 1) * RK],
                            Hf32.ap()[:, ex * RK:(ex + 1) * RK],
                            gbuf.ap()[:, ex:ex + 1], None, op0=ALU.mult),
                        inc="dve", waits=[("act", hev), ("dve", g_d)])
        t1 = p.do("tensor", lambda e: e.transpose(
            BB.ap()[0:112, 0:128], Hptm.ap()[:, 0:112], ident.ap()[:]),
            inc="pe", waits=[("dve", hp_d), ("act", htev.get(qt - 1, 0))])
        t2 = p.do("tensor", lambda e: e.transpose(
            BB.ap()[0:112, 128:256], Hptm.ap()[:, 112:224], ident.ap()[:]),
            inc="pe", waits=[("dve", hp_d)])
        ev1 = p.do("scalar", lambda e, qt=qt: e.copy(
            Ha.ap()[:, qt * 128:(qt + 1) * 128], BB.ap()[0:112, 0:128]),
            inc="act", waits=[("pe", t2)])
        ev2 = p.do("scalar", lambda e, qt=qt: e.copy(
            Hb.ap()[:, qt * 128:(qt + 1) * 128], BB.ap()[0:112, 128:256]),
            inc="act", waits=[("pe", t2)])
        htev[qt] = ev2

    p.barrier()

    maskb3 = maskb.ap().rearrange("p (t w) -> p t w", t=5)

    NT = NQ * NQT
    wqld = {}
    qmm_done = {}
    qevac = {}
    smm = {}
    mread = {}
    edone = {}
    ndone = {}
    tsn = {}
    tdone = {}
    pdone = {}
    vdone = {}
    odone = {}

    def st_qload(h):
        wqld[h] = p.do("gpsimd",
                       lambda e, h=h: e.dma_start(
                           out=wqstr3,
                           in_=wqT.rearrange("(c p) n -> p c n", p=128)[:, :, h * 128:(h + 1) * 128]),
                       dinc="ldw", waits=[("pe", qmm_done.get(h - 1, 0))])

    def st_qhalf(h, half):
        qmm = None
        for c in range(DC):
            qmm = p.do("tensor",
                       lambda e, c=c, half=half: e.matmul(
                           BF[0].ap()[:], wqstr3[:, c, :],
                           xnT3[:, c, half * 512:(half + 1) * 512],
                           start=(c == 0), stop=(c == DC - 1)),
                       inc="pe",
                       waits=[("ldw", wqld[h]),
                              ("act", qevac.get((h, half - 1), qevac.get((h - 1, 1), 0)))])
        qmm_done[h] = qmm
        qevac[(h, half)] = p.do("scalar",
                                lambda e, h=h, half=half: e.copy(
                                    Qsb[h % 2].ap()[:, half * 512:(half + 1) * 512],
                                    BF[0].ap()[:]),
                                inc="act",
                                waits=[("pe", max(qmm, smm.get((h - 2) * NQT + NQT - 1, 0)))])

    def st_S(i):
        h, qt = divmod(i, NQT)
        kvh = h // GQA
        par = i % 2
        SA, SB = BF[1 + par], BF[3 + par]
        mt = qt if qt < 4 else 4
        p.do("tensor",
             lambda e, qt=qt, kvh=kvh, SA=SA, h=h: e.matmul(
                 SA.ap()[:], Qsb[h % 2].ap()[:, qt * 128:(qt + 1) * 128],
                 K3[:, kvh, qt * 128:qt * 128 + 512], start=True, stop=False),
             inc="pe",
             waits=[("act", max(qevac[(h, 1)], edone.get(i - 2, 0))),
                    ("dve", mread.get(i - 2, 0)),
                    ("ldx", mask_ld)])
        p.do("tensor",
             lambda e, mt=mt, SA=SA: e.matmul(
                 SA.ap()[:], ident.ap()[:], maskb3[:, mt, 0:512],
                 start=False, stop=True), inc="pe")
        p.do("tensor",
             lambda e, qt=qt, kvh=kvh, SB=SB, h=h: e.matmul(
                 SB.ap()[:, 0:128], Qsb[h % 2].ap()[:, qt * 128:(qt + 1) * 128],
                 K3[:, kvh, qt * 128 + 512:qt * 128 + WIN], start=True, stop=False),
             inc="pe")
        smm[i] = p.do("tensor",
                      lambda e, mt=mt, SB=SB: e.matmul(
                          SB.ap()[:, 0:128], ident.ap()[:], maskb3[:, mt, 512:WIN],
                          start=False, stop=True), inc="pe")

    mnms = {}

    def st_M(i):
        par = i % 2
        SA, SB = BF[1 + par], BF[3 + par]
        cb = C_B(par)
        d = p.do("vector",
                 lambda e, SA=SA, cb=cb: e.reduce_max(
                     stap[:, cb:cb + 1], SA.ap()[:], axis=AX.X, negate=True),
                 inc="dve",
                 waits=[("pe", smm[i]), ("act", edone.get(i - 2, 0))])
        d = p.do("vector",
                 lambda e, SB=SB, cb=cb: e.reduce_max(
                     stap[:, cb + 1:cb + 2], SB.ap()[:, 0:128], axis=AX.X, negate=True),
                 inc="dve", waits=[("dve", d)])
        mread[i] = d
        mnms[i] = p.do("vector",
                       lambda e, cb=cb: e.tensor_scalar(
                           stap[:, cb + 2:cb + 3], stap[:, cb:cb + 1],
                           stap[:, cb + 1:cb + 2], SCALE, op0=ALU.min, op1=ALU.mult),
                       inc="dve", waits=[("dve", d)])

    def st_E(i):
        par = i % 2
        SA, SB = BF[1 + par], BF[3 + par]
        cb = C_B(par)
        p.do("scalar",
             lambda e, par=par, SA=SA, cb=cb: e.activation(
                 praw[par].ap()[:, 0:512], SA.ap()[:], AF.Exp,
                 bias=stap[:, cb + 2:cb + 3], scale=SCALE,
                 accum_out=stap[:, cb + 3:cb + 4]),
             inc="act",
             waits=[("dve", max(mnms[i], tsn.get(i - 2, 0)))])
        edone[i] = p.do("scalar",
                        lambda e, par=par, SB=SB, cb=cb: e.activation(
                            praw[par].ap()[:, 512:WIN], SB.ap()[:, 0:128], AF.Exp,
                            bias=stap[:, cb + 2:cb + 3], scale=SCALE,
                            accum_out=stap[:, cb + 4:cb + 5]),
                        inc="act", waits=[])

    def st_N(i):
        par = i % 2
        cb = C_B(par)
        d = p.do("vector",
                 lambda e, cb=cb: e.tensor_add(
                     stap[:, cb + 5:cb + 6], stap[:, cb + 3:cb + 4],
                     stap[:, cb + 4:cb + 5]),
                 inc="dve", waits=[("act", edone[i])])
        d = p.do("vector",
                 lambda e, cb=cb: e.reciprocal(
                     stap[:, cb + 6:cb + 7], stap[:, cb + 5:cb + 6]),
                 inc="dve", waits=[("dve", d)])
        ndone[i] = d
        tsn[i] = p.do("vector",
                      lambda e, par=par, cb=cb: e.tensor_scalar(
                          pm[par].ap()[:], praw[par].ap()[:],
                          stap[:, cb + 6:cb + 7], None, op0=ALU.mult),
                      inc="dve",
                      waits=[("dve", d), ("pe", tdone.get(i - 2, 0))])

    def st_T(i):
        par = i % 2
        d = None
        for c in range(5):
            d = p.do("tensor",
                     lambda e, c=c, par=par: e.transpose(
                         Bb[par].ap()[:, c * 128:(c + 1) * 128],
                         pm[par].ap()[:, c * 128:(c + 1) * 128], ident.ap()[:]),
                     inc="pe",
                     waits=[("dve", max(tsn[i], pdone.get(i - 2, 0)))])
        tdone[i] = d

    def st_P(i):
        par = i % 2
        pdone[i] = p.do("vector",
                        lambda e, par=par: e.tensor_copy(PTsb[par].ap()[:],
                                                         Bb[par].ap()[:, 0:WIN]),
                        inc="dve",
                        waits=[("pe", max(tdone[i], vdone.get(i - 2, 0)))])

    def st_V(i):
        h, qt = divmod(i, NQT)
        kvh = h // GQA
        par = i % 2
        d = None
        for c in range(5):
            d = p.do("tensor",
                     lambda e, c=c, qt=qt, kvh=kvh, par=par: e.matmul(
                         BF[5].ap()[:, 0:128],
                         V3[:, qt + c, kvh * 128:(kvh + 1) * 128],
                         PTsb[par].ap()[:, c * 128:(c + 1) * 128],
                         start=(c == 0), stop=(c == 4)),
                     inc="pe",
                     waits=[("dve", pdone[i]), ("act", odone.get(i - 1, 0))])
        vdone[i] = d

    def st_O(i):
        h, qt = divmod(i, NQT)
        odone[i] = p.do("scalar",
                        lambda e, h=h, qt=qt: e.copy(
                            O3[:, h, qt * 128:(qt + 1) * 128], BF[5].ap()[:, 0:128]),
                        inc="act", waits=[("pe", vdone[i])])

    st_qload(0)
    st_qhalf(0, 0)
    st_qhalf(0, 1)

    for i in range(NT + 3):
        if i < NT:
            r = i % NQT
            h = i // NQT
            if r == 2 and h + 1 < NQ:
                st_qload(h + 1)
            if r == 6 and h + 1 < NQ:
                st_qhalf(h + 1, 0)
            if r == 7 and h + 1 < NQ:
                st_qhalf(h + 1, 1)
            st_S(i)
        if 0 <= i - 1 < NT:
            st_M(i - 1)
            st_E(i - 1)
        if 0 <= i - 2 < NT:
            st_N(i - 2)
            st_T(i - 2)
            st_P(i - 2)
        if 0 <= i - 3 < NT:
            st_V(i - 3)
            st_O(i - 3)

    p.barrier()

    ocld = {}
    ocmm_last = {}
    lora_last = {}
    outadd = {}
    xldQ = {}
    stores = {}
    NJ = NOC * NQT

    def c_loads(oc):
        par = oc % 2
        w = [("pe", ocmm_last.get(oc - 2, 0))]
        p.do("gpsimd", lambda e, oc=oc, par=par: e.dma_start(
            out=bstr3[par],
            in_=baseT.rearrange("(c p) n -> p c n", p=128)[:, :, oc * OC:(oc + 1) * OC]),
            dinc="ldw", waits=w)
        return p.do("gpsimd", lambda e, oc=oc, par=par: e.dma_start(
            out=wstr3[par],
            in_=woT.rearrange("(c p) n -> p c n", p=128)[:, :, oc * OC:(oc + 1) * OC]),
            dinc="ldw", waits=w)

    btld = {}

    def bt_load(oc):
        btld[oc] = p.do("gpsimd", lambda e, oc=oc: e.dma_start(
            out=BTall.ap()[:].rearrange("p (g o) -> p g o", g=2),
            in_=lbT.rearrange("(g p) o -> p g o", p=112)[:, :, oc * OC:(oc + 1) * OC]),
            dinc="ldw", waits=[("pe", lora_last.get(oc - 1, 0))])

    def c_xload(k):
        oc, q2 = divmod(k, 4)
        return p.do("sync", lambda e, oc=oc, q2=q2, k=k: e.dma_start(
            out=xc2[k % 2].ap()[:].rearrange("p (g d) -> p g d", g=2),
            in_=xpad[HALO + q2 * 256:HALO + (q2 + 1) * 256,
                     oc * OC:(oc + 1) * OC].rearrange("(g p) d -> p g d", p=128)),
            dinc="ldx", waits=[("dve", outadd.get((k - 1) * 2 - 1, 0))])

    ocld[0] = c_loads(0)
    ocld[1] = c_loads(1)
    bt_load(0)
    xldQ[0] = c_xload(0)
    xldQ[1] = c_xload(1)

    for oc in range(NOC):
        par = oc % 2
        for qt in range(NQT):
            j = oc * NQT + qt
            k = oc * 4 + qt // 2
            PC = BF[j % 4]
            mm = None
            mw = [("ldw", ocld[oc]), ("dve", outadd.get(j - 4, 0))]
            for c in range(DC):
                mm = p.do("tensor", lambda e, c=c, qt=qt, par=par, PC=PC: e.matmul(
                    PC.ap()[:, 0:OC], xnT3[:, c, qt * 128:(qt + 1) * 128],
                    bstr3[par][:, c, :], start=(c == 0), stop=False),
                    inc="pe", waits=mw)
                mw = []
            mm = p.do("tensor", lambda e, qt=qt, PC=PC: e.matmul(
                PC.ap()[:, 0:OC], Ha.ap()[:, qt * 128:(qt + 1) * 128],
                BTall.ap()[:, 0:OC], start=False, stop=False), inc="pe",
                waits=[("ldw", btld[oc])])
            mm = p.do("tensor", lambda e, qt=qt, PC=PC: e.matmul(
                PC.ap()[:, 0:OC], Hb.ap()[:, qt * 128:(qt + 1) * 128],
                BTall.ap()[:, OC:2 * OC], start=False, stop=False), inc="pe")
            lora_last[oc] = mm
            for a in range(DC):
                mm = p.do("tensor", lambda e, a=a, qt=qt, par=par, PC=PC: e.matmul(
                    PC.ap()[:, 0:OC], O3[:, a, qt * 128:(qt + 1) * 128],
                    wstr3[par][:, a, :], start=False, stop=(a == DC - 1)),
                    inc="pe")
            ocmm_last[oc] = mm
            oa = p.do("vector", lambda e, j=j, qt=qt, k=k, PC=PC: e.tensor_add(
                outst[j % 2].ap()[:], PC.ap()[:, 0:OC],
                xc2[k % 2].ap()[:, (qt % 2) * OC:(qt % 2 + 1) * OC]),
                inc="dve",
                waits=[("pe", mm), ("ldx", xldQ[k]), ("st", 16 * max(0, j - 1))])
            outadd[j] = oa
            if qt % 2 == 1 and k + 2 < NOC * 4:
                xldQ[k + 2] = c_xload(k + 2)
            stores[j] = p.do("sync", lambda e, oc=oc, qt=qt, j=j: e.dma_start(
                out=out[qt * 128:(qt + 1) * 128, oc * OC:(oc + 1) * OC],
                in_=outst[j % 2].ap()[:]),
                dinc="st", waits=[("dve", oa)])
            if qt == NQT - 1 and oc + 1 < NOC:
                bt_load(oc + 1)
        if oc + 2 < NOC:
            ocld[oc + 2] = c_loads(oc + 2)

    p.do("sync", None, waits=[("st", 16 * NOC * NQT)])
    p.do("gpsimd", None, waits=[("st", 16 * NOC * NQT)])

    with nc.Block() as block:
        p.play(block)

    ctx.close()
    return nc


_NC_CACHE = None


def _get_nc():
    global _NC_CACHE
    if _NC_CACHE is None:
        _NC_CACHE = build_nc()
    return _NC_CACHE


def _host_inputs(L0_attn_x, L0_attn_wq, L0_attn_wk, L0_attn_wv, L0_attn_wo,
                 L0_ffn_base_W, L0_ffn_gate_W, L0_ffn_lora_A, L0_ffn_lora_B):
    xf = np.ascontiguousarray(np.asarray(L0_attn_x).reshape(8192, D))
    wqT = np.ascontiguousarray(np.asarray(L0_attn_wq).T)
    wkT = np.ascontiguousarray(np.asarray(L0_attn_wk).T)
    wvT = np.ascontiguousarray(np.asarray(L0_attn_wv).T)
    woT = np.ascontiguousarray(np.asarray(L0_attn_wo).T)
    baseT = np.ascontiguousarray(np.asarray(L0_ffn_base_W).T)
    gwT = np.ascontiguousarray(np.asarray(L0_ffn_gate_W).T)
    laT = np.ascontiguousarray(np.asarray(L0_ffn_lora_A).reshape(ER, D).T)
    lbT = np.ascontiguousarray(
        np.asarray(L0_ffn_lora_B).transpose(0, 2, 1).reshape(ER, D))
    ident = np.eye(128, dtype=ml_dtypes.bfloat16)

    r = np.arange(128)[:, None]
    c = np.arange(WIN)[None, :]
    band = (c - r >= 1) & (c - r <= 512)
    NEG = np.float32(-1e30)

    def mk(valid):
        return np.where(valid, np.float32(0.0), NEG)

    full5 = np.stack([mk(band)] * 5)
    m05 = np.stack([mk(band & (c >= HALO - qt * 128)) for qt in range(4)] + [mk(band)])
    masks = [np.ascontiguousarray((m05 if core == 0 else full5)
                                  .astype(ml_dtypes.bfloat16))
             for core in range(NCORES)]

    in_maps = []
    for core in range(NCORES):
        lo = core * TOWN
        if core == 0:
            xpad = np.concatenate(
                [np.zeros((HALO, D), np.float32), xf[0:TOWN]], axis=0)
        else:
            xpad = xf[lo - HALO: lo + TOWN]
        in_maps.append({
            "xpad": np.ascontiguousarray(xpad),
            "wqT": wqT, "wkT": wkT, "wvT": wvT, "woT": woT,
            "baseT": baseT, "gwT": gwT, "laT": laT, "lbT": lbT,
            "m01": masks[core], "ident": ident,
        })
    return in_maps


def kernel(L0_attn_x, L0_attn_wq, L0_attn_wk, L0_attn_wv, L0_attn_wo,
           L0_ffn_base_W, L0_ffn_gate_W, L0_ffn_lora_A, L0_ffn_lora_B):
    in_maps = _host_inputs(L0_attn_x, L0_attn_wq, L0_attn_wk, L0_attn_wv,
                           L0_attn_wo, L0_ffn_base_W, L0_ffn_gate_W,
                           L0_ffn_lora_A, L0_ffn_lora_B)
    nc = _get_nc()
    res = run_bass_kernel_spmd(nc, in_maps, core_ids=list(range(NCORES)))
    outp = np.concatenate([res.results[i]["out"] for i in range(NCORES)], axis=0)
    return outp.reshape(1, 8192, D)


# version 15
# speedup vs baseline: 1.1088x; 1.1088x over previous
import contextlib

import numpy as np
import ml_dtypes

import concourse.bass as bass
import concourse.mybir as mybir
from concourse.bass_utils import run_bass_kernel_spmd

F32 = mybir.dt.float32
BF16 = mybir.dt.bfloat16
AF = mybir.ActivationFunctionType
ALU = mybir.AluOpType
AX = mybir.AxisListType

NCORES = 8
TOWN = 1024
HALO = 512
TPAD = TOWN + HALO
D = 4096
DC = 32
NQ = 32
NKV = 8
GQA = 4
HD = 128
WIN = 640
NQT = 8
OC = 256
NOC = D // OC
NE = 28
RK = 8
ER = NE * RK
SCALE = 1.0 / float(np.sqrt(128.0))
EPS = 1e-5
ARENA = 211968

ENGS = ("sync", "gpsimd", "tensor", "vector", "scalar")


class Plan:

    def __init__(self, nc, ctx):
        self.nc = nc
        self.ops = {e: [] for e in ENGS}
        self.sems = {}
        self.cnt = {}
        self.hiw = {e: {} for e in ENGS}
        for name in ("ldx", "ldw", "st", "pe", "dve", "act", "pool"):
            self.sems[name] = ctx.enter_context(nc.semaphore(name))
            self.cnt[name] = 0

    def do(self, eng, fn, inc=None, dinc=None, waits=()):
        sname, amount = None, 0
        if inc is not None:
            sname, amount = inc, 1
        if dinc is not None:
            sname, amount = dinc, 16
        eff = []
        hw = self.hiw[eng]
        for (n, v) in waits:
            if v > hw.get(n, 0):
                hw[n] = v
                eff.append((n, v))
        if sname is not None:
            self.cnt[sname] += amount
        ret = self.cnt[sname] if sname is not None else None
        sems = self.sems

        def closure(e, fn=fn, eff=tuple(eff), sname=sname, amount=amount):
            for (n, v) in eff:
                e.wait_ge(sems[n], v)
            if fn is not None:
                inst = fn(e)
                if sname is not None:
                    inst.then_inc(sems[sname], amount)

        self.ops[eng].append(closure)
        return ret

    def barrier(self):
        snap = dict(self.cnt)
        for e in ENGS:
            self.do(e, None, waits=list(snap.items()))

    def play(self, block):
        for e in ENGS:
            lst = self.ops[e]
            if lst:
                getattr(block, e)(lambda eng, lst=lst: [f(eng) for f in lst])


DEBUG_XNT = False


def build_nc():
    nc = bass.Bass("TRN2", target_bir_lowering=False)

    xpad = nc.dram_tensor("xpad", [TPAD, D], F32, kind="ExternalInput").ap()
    wqT = nc.dram_tensor("wqT", [D, NQ * HD], F32, kind="ExternalInput").ap()
    wkT = nc.dram_tensor("wkT", [D, NKV * HD], F32, kind="ExternalInput").ap()
    wvT = nc.dram_tensor("wvT", [D, NKV * HD], F32, kind="ExternalInput").ap()
    woT = nc.dram_tensor("woT", [D, D], F32, kind="ExternalInput").ap()
    baseT = nc.dram_tensor("baseT", [D, D], F32, kind="ExternalInput").ap()
    gwT = nc.dram_tensor("gwT", [D, NE], F32, kind="ExternalInput").ap()
    laT = nc.dram_tensor("laT", [D, ER], F32, kind="ExternalInput").ap()
    lbT = nc.dram_tensor("lbT", [ER, D], F32, kind="ExternalInput").ap()
    m01 = nc.dram_tensor("m01", [5, 128, WIN], BF16, kind="ExternalInput").ap()
    ident_i = nc.dram_tensor("ident", [128, 128], BF16, kind="ExternalInput").ap()
    out = nc.dram_tensor("out", [TOWN, D], F32, kind="ExternalOutput").ap()
    if DEBUG_XNT:
        dbg_xnT = nc.dram_tensor("dbg_xnT", [128, DC * TOWN], F32, kind="ExternalOutput").ap()
        dbg_halo = nc.dram_tensor("dbg_halo", [128, DC * HALO], F32, kind="ExternalOutput").ap()

    ctx = contextlib.ExitStack()
    arena = ctx.enter_context(nc.sbuf_tensor("arena", [128, ARENA], mybir.dt.uint8))
    arena_addr = nc.lookup_mloc(arena).addr

    def at(name, shape, dtype, off):
        h = nc.virtual_sbuf_tensor(name, shape, dtype)
        nc._tensor(h.name, h.shape, h.dtype, type="SB")
        mloc = nc.lookup_mloc(h)
        mloc.allocated = not nc.target_bir_lowering
        mloc.addr = arena_addr + off
        return h

    xnT = at("xnT", [128, DC * TOWN], BF16, 0)
    ident = at("ident_sb", [128, 128], BF16, 65536)
    st = at("stats", [128, 256], F32, 65792)
    Ha = at("Ha", [112, TOWN], BF16, 66816)
    Hb = at("Hb", [112, TOWN], BF16, 68864)
    O = at("O", [128, DC * TOWN], BF16, 70912)
    haloT = at("haloT", [128, DC * HALO], BF16, 70912)
    xstg = [at("xstg0", [128, D], F32, 103680),
            at("xstg1", [128, D], F32, 120064),
            at("xstg2", [128, D], F32, 152832)]
    wvstr = [at("wvstr0", [128, DC * 256], BF16, 103680),
             at("wvstr1", [128, DC * 256], BF16, 120064)]
    K = at("K", [128, NKV * TPAD], BF16, 136448)
    V = at("V", [128, 12 * TOWN], BF16, 161024)
    sqbuf = at("sqbuf", [128, D], F32, 136448)
    xn_tm = [at("xn_tm0", [128, D], BF16, 185600),
             at("xn_tm1", [128, D], BF16, 193792)]
    wkstr = [at("wkstr0", [128, DC * 128], BF16, 185600),
             at("wkstr1", [128, DC * 128], BF16, 193792)]
    ATstr = at("ATstr", [128, DC * ER], BF16, 185600)
    gwstr = at("gwstr", [128, DC * NE], BF16, 199936)
    Hf32 = at("Hf32", [128, ER], F32, 201728)
    Hptm = at("Hptm", [128, ER], BF16, 202624)
    gbuf = at("gbuf", [128, NE], F32, 203072)
    eq1b = at("eq1b", [128, NE], F32, 203200)
    eq2b = at("eq2b", [128, NE], F32, 203328)
    mkb = at("mkb", [128, NE], F32, 203456)
    wqstr = at("wqstr", [128, DC * 128], BF16, 185600)
    Qsb = [at("Qsb0", [128, TOWN], BF16, 193792),
           at("Qsb1", [128, TOWN], BF16, 195840)]
    praw = [at("praw0", [128, WIN], BF16, 197888),
            at("praw1", [128, WIN], BF16, 199168)]
    pm = [at("pm0", [128, WIN], BF16, 200448),
          at("pm1", [128, WIN], BF16, 201728)]
    PTsb = [at("PTsb0", [128, WIN], BF16, 203008),
            at("PTsb1", [128, WIN], BF16, 204288)]
    maskb = at("maskb", [128, 5 * WIN], BF16, 205568)
    bstr = [at("bstr0", [128, DC * OC], BF16, 136448),
            at("bstr1", [128, DC * OC], BF16, 152832)]
    wstr = [at("wstr0", [128, DC * OC], BF16, 169216),
            at("wstr1", [128, DC * OC], BF16, 185600)]
    xc2 = [at("xc20", [128, 2 * OC], F32, 201984),
           at("xc21", [128, 2 * OC], F32, 204032)]
    BTall = at("BTall", [112, 2 * OC], BF16, 206080)
    outst = [at("outst0", [128, OC], F32, 208128), at("outst1", [128, OC], F32, 209152)]

    xnT3 = xnT.ap().rearrange("p (c n) -> p c n", c=DC)
    haloT3 = haloT.ap().rearrange("p (c n) -> p c n", c=DC)
    K3 = K.ap().rearrange("p (h n) -> p h n", h=NKV)
    V3 = V.ap().rearrange("p (t n) -> p t n", t=12)
    O3 = O.ap().rearrange("p (c n) -> p c n", c=DC)
    ATstr3 = ATstr.ap().rearrange("p (c n) -> p c n", c=DC)
    gwstr3 = gwstr.ap().rearrange("p (c n) -> p c n", c=DC)
    wqstr3 = wqstr.ap().rearrange("p (c n) -> p c n", c=DC)
    wvstr3 = [w.ap().rearrange("p (c n) -> p c n", c=DC) for w in wvstr]
    wkstr3 = [w.ap().rearrange("p (c n) -> p c n", c=DC) for w in wkstr]
    bstr3 = [w.ap().rearrange("p (c n) -> p c n", c=DC) for w in bstr]
    wstr3 = [w.ap().rearrange("p (c n) -> p c n", c=DC) for w in wstr]

    BF = [ctx.enter_context(nc.psum_tensor(f"psF{i}", [128, 512], F32)) for i in range(6)]
    Bb = [ctx.enter_context(nc.psum_tensor(f"psB{i}", [128, 1024], BF16)) for i in range(2)]
    BB = Bb[0]

    p = Plan(nc, ctx)

    C_EPS = 0
    C_A = lambda tt: 8 + (tt % 2) * 8
    C_M1, C_M2, C_DM, C_P1, C_P2 = 24, 25, 26, 27, 28
    C_B = lambda par: 32 + par * 8
    C_SUM = lambda par: 48 + par
    C_RCP = lambda par: 50 + par

    stap = st.ap()

    p.do("vector", lambda e: e.memset(stap[:, C_EPS:C_EPS + 1], EPS), inc="dve")
    ld_id = p.do("sync", lambda e: e.dma_start(out=ident.ap()[:], in_=ident_i[:]), dinc="ldx")

    NTT = TPAD // 128
    xn_done = {}
    var_done = {}
    trevac = {}
    ldx_t = {}
    sq_done = {}
    sum_done = {}

    def a_load(tt):
        w = [("dve", xn_done.get(tt - 3, 0))]
        ldx_t[tt] = p.do("sync",
                         lambda e, tt=tt: e.dma_start(
                             out=xstg[tt % 3].ap()[:],
                             in_=xpad[tt * 128:(tt + 1) * 128, :]),
                         dinc="ldx", waits=w)

    def a_stats1(tt):
        ca = C_A(tt)
        sq_done[tt] = p.do("scalar",
                           lambda e, tt=tt, ca=ca: e.activation(
                               sqbuf.ap()[:], xstg[tt % 3].ap()[:], AF.Square,
                               accum_out=stap[:, ca + 1:ca + 2]),
                           inc="act",
                           waits=[("ldx", ldx_t[tt]), ("dve", var_done.get(tt - 2, 0))])
        if tt % 2 == 0:
            sum_done[tt] = ("dve", p.do("vector",
                            lambda e, tt=tt, ca=ca: e.reduce_sum(
                                stap[:, ca:ca + 1], xstg[tt % 3].ap()[:], axis=AX.X),
                            inc="dve", waits=[("ldx", ldx_t[tt])]))
        else:
            sum_done[tt] = ("act", p.do("scalar",
                            lambda e, tt=tt, ca=ca: e.activation(
                                sqbuf.ap()[:], xstg[tt % 3].ap()[:], AF.Copy,
                                accum_out=stap[:, ca:ca + 1]),
                            inc="act", waits=[("ldx", ldx_t[tt])]))

    def a_stats2(tt):
        ca = C_A(tt)
        d = p.do("vector",
                 lambda e, ca=ca: e.tensor_scalar(
                     stap[:, ca + 2:ca + 3], stap[:, ca:ca + 1], -1.0 / D, None,
                     op0=ALU.mult),
                 inc="dve", waits=[sum_done[tt]])
        d = p.do("vector",
                 lambda e, ca=ca: e.tensor_scalar(
                     stap[:, ca + 3:ca + 4], stap[:, ca + 2:ca + 3],
                     stap[:, ca + 2:ca + 3], None, op0=ALU.mult),
                 inc="dve", waits=[("dve", d)])
        v_d = p.do("vector",
                   lambda e, ca=ca: e.scalar_tensor_tensor(
                       stap[:, ca + 4:ca + 5], stap[:, ca + 1:ca + 2], 1.0 / D,
                       stap[:, ca + 3:ca + 4], op0=ALU.mult, op1=ALU.subtract),
                   inc="dve", waits=[("act", sq_done[tt]), ("dve", d)])
        var_done[tt] = v_d
        d = p.do("scalar",
                 lambda e, ca=ca: e.activation(
                     stap[:, ca + 5:ca + 6], stap[:, ca + 4:ca + 5], AF.Ln,
                     bias=stap[:, C_EPS:C_EPS + 1]),
                 inc="act", waits=[("dve", v_d)])
        rs_d = p.do("scalar",
                    lambda e, ca=ca: e.activation(
                        stap[:, ca + 6:ca + 7], stap[:, ca + 5:ca + 6], AF.Exp,
                        scale=-0.5),
                    inc="act", waits=[("act", d)])
        xn_done[tt] = p.do("vector",
                           lambda e, tt=tt, ca=ca: e.tensor_scalar(
                               xn_tm[tt % 2].ap()[:], xstg[tt % 3].ap()[:],
                               stap[:, ca + 2:ca + 3], stap[:, ca + 6:ca + 7],
                               op0=ALU.add, op1=ALU.mult),
                           inc="dve",
                           waits=[("act", rs_d), ("pe", trans_done.get(tt - 2, 0))])

    trans_done = {}

    def a_trans(tt):
        for g in range(8):
            gi = tt * 8 + g
            prev_ev = trevac.get(gi - 2, 0)
            esem = "act" if g % 2 == 0 else "dve"
            tr_d = None
            for k in range(4):
                c = 4 * g + k
                tr_d = p.do("tensor",
                            lambda e, c=c, k=k, g=g, tt=tt: e.transpose(
                                Bb[g % 2].ap()[:, k * 128:(k + 1) * 128],
                                xn_tm[tt % 2].ap()[:, c * 128:(c + 1) * 128],
                                ident.ap()[:]),
                            inc="pe",
                            waits=[("dve", xn_done[tt]), (esem, prev_ev),
                                   ("ldx", ld_id)])
            if tt < 4:
                dst = haloT3[:, 4 * g:4 * g + 4, tt * 128:(tt + 1) * 128]
            else:
                dst = xnT3[:, 4 * g:4 * g + 4, (tt - 4) * 128:(tt - 3) * 128]
            if g % 2 == 0:
                ev = p.do("scalar",
                          lambda e, dst=dst, g=g: e.copy(
                              dst, Bb[g % 2].ap()[:, 0:512].rearrange("p (k n) -> p k n", k=4)),
                          inc="act", waits=[("pe", tr_d)])
            else:
                ev = p.do("vector",
                          lambda e, dst=dst, g=g: e.tensor_copy(
                              dst, Bb[g % 2].ap()[:, 0:512].rearrange("p (k n) -> p k n", k=4)),
                          inc="dve", waits=[("pe", tr_d)])
            trevac[gi] = ev
        trans_done[tt] = p.cnt["pe"]

    a_load(0)
    a_load(1)
    for tt in range(NTT + 2):
        if tt < NTT:
            a_stats1(tt)
        if 0 <= tt - 1 < NTT:
            a_stats2(tt - 1)
        if tt + 2 < NTT:
            a_load(tt + 2)
        if 0 <= tt - 2 < NTT:
            a_trans(tt - 2)

    p.barrier()

    if DEBUG_XNT:
        dzc = {}
        for half in range(16):
            dzc[half] = p.do("vector", lambda e, half=half: e.tensor_copy(
                xstg[0].ap()[:, 0:2048].rearrange("p (a b) -> p a b", a=1),
                xnT.ap()[:, half * 2048:(half + 1) * 2048].rearrange("p (a b) -> p a b", a=1)),
                inc="dve")
            p.do("sync", lambda e, half=half: e.dma_start(
                out=dbg_xnT[:, half * 2048:(half + 1) * 2048], in_=xstg[0].ap()[:, 0:2048]),
                dinc="st", waits=[("dve", dzc[half])])
            p.barrier()
        for half in range(8):
            dzc[half] = p.do("vector", lambda e, half=half: e.tensor_copy(
                xstg[0].ap()[:, 0:2048],
                haloT.ap()[:, half * 2048:(half + 1) * 2048]),
                inc="dve")
            p.do("sync", lambda e, half=half: e.dma_start(
                out=dbg_halo[:, half * 2048:(half + 1) * 2048], in_=xstg[0].ap()[:, 0:2048]),
                dinc="st", waits=[("dve", dzc[half])])
            p.barrier()

    def slab_rhs(s, width, c):
        if s == 0:
            return haloT3[:, c, 0:width]
        return xnT3[:, c, (s - 1) * 512:(s - 1) * 512 + width]

    kld = {}
    kmm = {}
    kevac = {}

    def k_load(kd):
        return p.do("gpsimd",
                    lambda e, kd=kd: e.dma_start(
                        out=wkstr3[kd % 2],
                        in_=wkT.rearrange("(c p) n -> p c n", p=128)[:, :, kd * 128:(kd + 1) * 128]),
                    dinc="ldw", waits=[("pe", kmm.get(kd - 2, 0))])

    kld[0] = k_load(0)
    kld[1] = k_load(1)
    for kd in range(NKV):
        for s in range(3):
            ps = BF[(kd * 3 + s) % 2]
            mm = None
            for c in range(DC):
                mm = p.do("tensor",
                          lambda e, kd=kd, s=s, c=c, ps=ps: e.matmul(
                              ps.ap()[:], wkstr3[kd % 2][:, c, :], slab_rhs(s, 512, c),
                              start=(c == 0), stop=(c == DC - 1)),
                          inc="pe",
                          waits=[("ldw", kld[kd]),
                                 ("dve", kevac.get((kd * 3 + s) - 2, 0))])
            kmm[kd] = mm
            kevac[kd * 3 + s] = p.do(
                "vector",
                lambda e, kd=kd, s=s, ps=ps: e.tensor_copy(
                    K3[:, kd, s * 512:(s + 1) * 512], ps.ap()[:]),
                inc="dve", waits=[("pe", mm)])
        if kd + 2 < NKV:
            kld[kd + 2] = k_load(kd + 2)

    at_ld = p.do("gpsimd",
                 lambda e: e.dma_start(out=ATstr3,
                                       in_=laT.rearrange("(c p) n -> p c n", p=128)),
                 dinc="ldw", waits=[("pe", kmm[NKV - 1])])
    gw_ld = p.do("gpsimd",
                 lambda e: e.dma_start(out=gwstr3,
                                       in_=gwT.rearrange("(c p) n -> p c n", p=128)),
                 dinc="ldw")

    vld = {}
    vmm = {}
    vevac = {}

    def v_load(q):
        return p.do("gpsimd",
                    lambda e, q=q: e.dma_start(
                        out=wvstr3[q % 2],
                        in_=wvT.rearrange("(c p) n -> p c n", p=128)[:, :, q * 256:(q + 1) * 256]),
                    dinc="ldw", waits=[("pe", vmm.get(q - 2, 0))])

    vld[0] = v_load(0)
    vld[1] = v_load(1)
    vi = 0
    for q in range(4):
        last_mm_q = None
        for tt in range(12):
            ps = BF[2 + vi % 2]
            if tt < 4:
                lhsT = lambda c, tt=tt: haloT3[:, c, tt * 128:(tt + 1) * 128]
            else:
                lhsT = lambda c, tt=tt: xnT3[:, c, (tt - 4) * 128:(tt - 3) * 128]
            mm = None
            for c in range(DC):
                mm = p.do("tensor",
                          lambda e, c=c, ps=ps, lhsT=lhsT, q=q: e.matmul(
                              ps.ap()[:, 0:256], lhsT(c), wvstr3[q % 2][:, c, :],
                              start=(c == 0), stop=(c == DC - 1)),
                          inc="pe",
                          waits=[("ldw", vld[q]), ("dve", vevac.get(vi - 2, 0))])
            last_mm_q = mm
            vevac[vi] = p.do("vector",
                             lambda e, tt=tt, q=q, ps=ps: e.tensor_copy(
                                 V3[:, tt, q * 256:(q + 1) * 256], ps.ap()[:, 0:256]),
                             inc="dve", waits=[("pe", mm)])
            vi += 1
        vmm[q] = last_mm_q
        if q + 2 < 4:
            vld[q + 2] = v_load(q + 2)

    p.barrier()

    mask_ld = p.do("sync",
                   lambda e: e.dma_start(
                       out=maskb.ap()[:].rearrange("p (t w) -> p t w", t=5),
                       in_=m01.rearrange("t p w -> p t w")),
                   dinc="ldx")
    lg_read = 0
    hevac_prev = 0
    htev = {}
    for qt in range(NQT):
        LG = BF[0]
        HM = BF[1]
        mm = None
        for c in range(DC):
            mm = p.do("tensor",
                      lambda e, c=c, qt=qt, LG=LG: e.matmul(
                          LG.ap()[:, 0:NE], xnT3[:, c, qt * 128:(qt + 1) * 128],
                          gwstr3[:, c, :], start=(c == 0), stop=(c == DC - 1)),
                      inc="pe", waits=[("ldw", gw_ld), ("dve", lg_read)])
        m1_d = p.do("vector", lambda e, LG=LG: e.reduce_max(
            stap[:, C_M1:C_M1 + 1], LG.ap()[:, 0:NE], axis=AX.X),
            inc="dve", waits=[("pe", mm)])
        e1_d = p.do("vector", lambda e, LG=LG: e.tensor_scalar(
            eq1b.ap()[:], LG.ap()[:, 0:NE], stap[:, C_M1:C_M1 + 1], None,
            op0=ALU.is_equal), inc="dve", waits=[("dve", m1_d)])
        mk_d = p.do("vector", lambda e, LG=LG: e.scalar_tensor_tensor(
            mkb.ap()[:], eq1b.ap()[:], -1e30, LG.ap()[:, 0:NE],
            op0=ALU.mult, op1=ALU.add), inc="dve", waits=[("dve", e1_d)])
        lg_read = mk_d
        m2_d = p.do("vector", lambda e: e.reduce_max(
            stap[:, C_M2:C_M2 + 1], mkb.ap()[:], axis=AX.X),
            inc="dve", waits=[("dve", mk_d)])
        e2_d = p.do("vector", lambda e: e.tensor_scalar(
            eq2b.ap()[:], mkb.ap()[:], stap[:, C_M2:C_M2 + 1], None,
            op0=ALU.is_equal), inc="dve", waits=[("dve", m2_d)])
        dm_d = p.do("vector", lambda e: e.tensor_tensor(
            stap[:, C_DM:C_DM + 1], stap[:, C_M2:C_M2 + 1],
            stap[:, C_M1:C_M1 + 1], op=ALU.subtract),
            inc="dve", waits=[("dve", e2_d)])
        p1_d = p.do("scalar", lambda e: e.activation(
            stap[:, C_P1:C_P1 + 1], stap[:, C_DM:C_DM + 1], AF.Sigmoid, scale=-1.0),
            inc="act", waits=[("dve", dm_d)])
        p2_d = p.do("scalar", lambda e: e.activation(
            stap[:, C_P2:C_P2 + 1], stap[:, C_DM:C_DM + 1], AF.Sigmoid),
            inc="act", waits=[("act", p1_d)])
        g1_d = p.do("vector", lambda e: e.tensor_scalar(
            gbuf.ap()[:], eq1b.ap()[:], stap[:, C_P1:C_P1 + 1], 2.0,
            op0=ALU.mult, op1=ALU.mult), inc="dve", waits=[("act", p2_d)])
        g2_d = p.do("vector", lambda e: e.tensor_scalar(
            eq2b.ap()[:], eq2b.ap()[:], stap[:, C_P2:C_P2 + 1], 2.0,
            op0=ALU.mult, op1=ALU.mult), inc="dve", waits=[("dve", g1_d)])
        g_d = p.do("vector", lambda e: e.tensor_add(
            gbuf.ap()[:], gbuf.ap()[:], eq2b.ap()[:]),
            inc="dve", waits=[("dve", g2_d)])

        mm2 = None
        for c in range(DC):
            mm2 = p.do("tensor",
                       lambda e, c=c, qt=qt, HM=HM: e.matmul(
                           HM.ap()[:, 0:ER], xnT3[:, c, qt * 128:(qt + 1) * 128],
                           ATstr3[:, c, :], start=(c == 0), stop=(c == DC - 1)),
                       inc="pe", waits=[("ldw", at_ld), ("act", hevac_prev)])
        hev = p.do("scalar", lambda e, HM=HM: e.copy(Hf32.ap()[:], HM.ap()[:, 0:ER]),
                   inc="act", waits=[("pe", mm2)])
        hevac_prev = hev
        hp_d = None
        for ex in range(NE):
            hp_d = p.do("vector",
                        lambda e, ex=ex: e.tensor_scalar(
                            Hptm.ap()[:, ex * RK:(ex + 1) * RK],
                            Hf32.ap()[:, ex * RK:(ex + 1) * RK],
                            gbuf.ap()[:, ex:ex + 1], None, op0=ALU.mult),
                        inc="dve", waits=[("act", hev), ("dve", g_d)])
        t1 = p.do("tensor", lambda e: e.transpose(
            BB.ap()[0:112, 0:128], Hptm.ap()[:, 0:112], ident.ap()[:]),
            inc="pe", waits=[("dve", hp_d), ("act", htev.get(qt - 1, 0))])
        t2 = p.do("tensor", lambda e: e.transpose(
            BB.ap()[0:112, 128:256], Hptm.ap()[:, 112:224], ident.ap()[:]),
            inc="pe", waits=[("dve", hp_d)])
        ev1 = p.do("scalar", lambda e, qt=qt: e.copy(
            Ha.ap()[:, qt * 128:(qt + 1) * 128], BB.ap()[0:112, 0:128]),
            inc="act", waits=[("pe", t2)])
        ev2 = p.do("scalar", lambda e, qt=qt: e.copy(
            Hb.ap()[:, qt * 128:(qt + 1) * 128], BB.ap()[0:112, 128:256]),
            inc="act", waits=[("pe", t2)])
        htev[qt] = ev2

    p.barrier()

    maskb3 = maskb.ap().rearrange("p (t w) -> p t w", t=5)

    NT = NQ * NQT
    wqld = {}
    qmm_done = {}
    qevac = {}
    smm = {}
    mread = {}
    edone = {}
    ndone = {}
    tsn = {}
    tdone = {}
    pdone = {}
    vdone = {}
    odone = {}

    def st_qload(h):
        wqld[h] = p.do("gpsimd",
                       lambda e, h=h: e.dma_start(
                           out=wqstr3,
                           in_=wqT.rearrange("(c p) n -> p c n", p=128)[:, :, h * 128:(h + 1) * 128]),
                       dinc="ldw", waits=[("pe", qmm_done.get(h - 1, 0))])

    def st_qhalf(h, half):
        qmm = None
        for c in range(DC):
            qmm = p.do("tensor",
                       lambda e, c=c, half=half: e.matmul(
                           BF[0].ap()[:], wqstr3[:, c, :],
                           xnT3[:, c, half * 512:(half + 1) * 512],
                           start=(c == 0), stop=(c == DC - 1)),
                       inc="pe",
                       waits=[("ldw", wqld[h]),
                              ("act", qevac.get((h, half - 1), qevac.get((h - 1, 1), 0)))])
        qmm_done[h] = qmm
        qevac[(h, half)] = p.do("scalar",
                                lambda e, h=h, half=half: e.copy(
                                    Qsb[h % 2].ap()[:, half * 512:(half + 1) * 512],
                                    BF[0].ap()[:]),
                                inc="act",
                                waits=[("pe", max(qmm, smm.get((h - 2) * NQT + NQT - 1, 0)))])

    def st_S(i):
        h, qt = divmod(i, NQT)
        kvh = h // GQA
        par = i % 2
        SA, SB = BF[1 + par], BF[3 + par]
        mt = qt if qt < 4 else 4
        p.do("tensor",
             lambda e, qt=qt, kvh=kvh, SA=SA, h=h: e.matmul(
                 SA.ap()[:], Qsb[h % 2].ap()[:, qt * 128:(qt + 1) * 128],
                 K3[:, kvh, qt * 128:qt * 128 + 512], start=True, stop=False),
             inc="pe",
             waits=[("act", max(qevac[(h, 1)], edone.get(i - 2, 0))),
                    ("dve", mread.get(i - 2, 0)),
                    ("ldx", mask_ld)])
        p.do("tensor",
             lambda e, mt=mt, SA=SA: e.matmul(
                 SA.ap()[:], ident.ap()[:], maskb3[:, mt, 0:512],
                 start=False, stop=True), inc="pe")
        p.do("tensor",
             lambda e, qt=qt, kvh=kvh, SB=SB, h=h: e.matmul(
                 SB.ap()[:, 0:128], Qsb[h % 2].ap()[:, qt * 128:(qt + 1) * 128],
                 K3[:, kvh, qt * 128 + 512:qt * 128 + WIN], start=True, stop=False),
             inc="pe")
        smm[i] = p.do("tensor",
                      lambda e, mt=mt, SB=SB: e.matmul(
                          SB.ap()[:, 0:128], ident.ap()[:], maskb3[:, mt, 512:WIN],
                          start=False, stop=True), inc="pe")

    mnms = {}

    def st_M(i):
        par = i % 2
        SA, SB = BF[1 + par], BF[3 + par]
        cb = C_B(par)
        d = p.do("vector",
                 lambda e, SA=SA, cb=cb: e.reduce_max(
                     stap[:, cb:cb + 1], SA.ap()[:], axis=AX.X, negate=True),
                 inc="dve",
                 waits=[("pe", smm[i]), ("act", edone.get(i - 2, 0))])
        d = p.do("vector",
                 lambda e, SB=SB, cb=cb: e.reduce_max(
                     stap[:, cb + 1:cb + 2], SB.ap()[:, 0:128], axis=AX.X, negate=True),
                 inc="dve", waits=[("dve", d)])
        mread[i] = d
        mnms[i] = p.do("vector",
                       lambda e, cb=cb: e.tensor_scalar(
                           stap[:, cb + 2:cb + 3], stap[:, cb:cb + 1],
                           stap[:, cb + 1:cb + 2], SCALE, op0=ALU.min, op1=ALU.mult),
                       inc="dve", waits=[("dve", d)])

    def st_E(i):
        par = i % 2
        SA, SB = BF[1 + par], BF[3 + par]
        cb = C_B(par)
        p.do("scalar",
             lambda e, par=par, SA=SA, cb=cb: e.activation(
                 praw[par].ap()[:, 0:512], SA.ap()[:], AF.Exp,
                 bias=stap[:, cb + 2:cb + 3], scale=SCALE,
                 accum_out=stap[:, cb + 3:cb + 4]),
             inc="act",
             waits=[("dve", max(mnms[i], tsn.get(i - 2, 0)))])
        edone[i] = p.do("scalar",
                        lambda e, par=par, SB=SB, cb=cb: e.activation(
                            praw[par].ap()[:, 512:WIN], SB.ap()[:, 0:128], AF.Exp,
                            bias=stap[:, cb + 2:cb + 3], scale=SCALE,
                            accum_out=stap[:, cb + 4:cb + 5]),
                        inc="act", waits=[])

    def st_N(i):
        par = i % 2
        cb = C_B(par)
        d = p.do("vector",
                 lambda e, cb=cb: e.tensor_add(
                     stap[:, cb + 5:cb + 6], stap[:, cb + 3:cb + 4],
                     stap[:, cb + 4:cb + 5]),
                 inc="dve", waits=[("act", edone[i])])
        d = p.do("vector",
                 lambda e, cb=cb: e.reciprocal(
                     stap[:, cb + 6:cb + 7], stap[:, cb + 5:cb + 6]),
                 inc="dve", waits=[("dve", d)])
        ndone[i] = d
        tsn[i] = p.do("vector",
                      lambda e, par=par, cb=cb: e.tensor_scalar(
                          pm[par].ap()[:], praw[par].ap()[:],
                          stap[:, cb + 6:cb + 7], None, op0=ALU.mult),
                      inc="dve",
                      waits=[("dve", d), ("pe", tdone.get(i - 2, 0))])

    def st_T(i):
        par = i % 2
        d = None
        for c in range(5):
            d = p.do("tensor",
                     lambda e, c=c, par=par: e.transpose(
                         Bb[par].ap()[:, c * 128:(c + 1) * 128],
                         pm[par].ap()[:, c * 128:(c + 1) * 128], ident.ap()[:]),
                     inc="pe",
                     waits=[("dve", tsn[i]), ("act", pdone.get(i - 2, 0))])
        tdone[i] = d

    def st_P(i):
        par = i % 2
        pdone[i] = p.do("scalar",
                        lambda e, par=par: e.copy(PTsb[par].ap()[:],
                                                  Bb[par].ap()[:, 0:WIN]),
                        inc="act",
                        waits=[("pe", max(tdone[i], vdone.get(i - 2, 0)))])

    def st_V(i):
        h, qt = divmod(i, NQT)
        kvh = h // GQA
        par = i % 2
        d = None
        for c in range(5):
            d = p.do("tensor",
                     lambda e, c=c, qt=qt, kvh=kvh, par=par: e.matmul(
                         BF[5].ap()[:, 0:128],
                         V3[:, qt + c, kvh * 128:(kvh + 1) * 128],
                         PTsb[par].ap()[:, c * 128:(c + 1) * 128],
                         start=(c == 0), stop=(c == 4)),
                     inc="pe",
                     waits=[("act", max(pdone[i], odone.get(i - 1, 0)))])
        vdone[i] = d

    def st_O(i):
        h, qt = divmod(i, NQT)
        odone[i] = p.do("scalar",
                        lambda e, h=h, qt=qt: e.copy(
                            O3[:, h, qt * 128:(qt + 1) * 128], BF[5].ap()[:, 0:128]),
                        inc="act", waits=[("pe", vdone[i])])

    st_qload(0)
    st_qhalf(0, 0)
    st_qhalf(0, 1)

    for i in range(NT + 3):
        if i < NT:
            r = i % NQT
            h = i // NQT
            if r == 2 and h + 1 < NQ:
                st_qload(h + 1)
            if r == 6 and h + 1 < NQ:
                st_qhalf(h + 1, 0)
            if r == 7 and h + 1 < NQ:
                st_qhalf(h + 1, 1)
            st_S(i)
        if 0 <= i - 1 < NT:
            st_M(i - 1)
            st_E(i - 1)
        if 0 <= i - 2 < NT:
            st_N(i - 2)
            st_T(i - 2)
            st_P(i - 2)
        if 0 <= i - 3 < NT:
            st_V(i - 3)
            st_O(i - 3)

    p.barrier()

    ocld = {}
    ocmm_last = {}
    lora_last = {}
    outadd = {}
    xldQ = {}
    stores = {}
    NJ = NOC * NQT

    def c_loads(oc):
        par = oc % 2
        w = [("pe", ocmm_last.get(oc - 2, 0))]
        p.do("gpsimd", lambda e, oc=oc, par=par: e.dma_start(
            out=bstr3[par],
            in_=baseT.rearrange("(c p) n -> p c n", p=128)[:, :, oc * OC:(oc + 1) * OC]),
            dinc="ldw", waits=w)
        return p.do("gpsimd", lambda e, oc=oc, par=par: e.dma_start(
            out=wstr3[par],
            in_=woT.rearrange("(c p) n -> p c n", p=128)[:, :, oc * OC:(oc + 1) * OC]),
            dinc="ldw", waits=w)

    btld = {}

    def bt_load(oc):
        btld[oc] = p.do("gpsimd", lambda e, oc=oc: e.dma_start(
            out=BTall.ap()[:].rearrange("p (g o) -> p g o", g=2),
            in_=lbT.rearrange("(g p) o -> p g o", p=112)[:, :, oc * OC:(oc + 1) * OC]),
            dinc="ldw", waits=[("pe", lora_last.get(oc - 1, 0))])

    def c_xload(k):
        oc, q2 = divmod(k, 4)
        return p.do("sync", lambda e, oc=oc, q2=q2, k=k: e.dma_start(
            out=xc2[k % 2].ap()[:].rearrange("p (g d) -> p g d", g=2),
            in_=xpad[HALO + q2 * 256:HALO + (q2 + 1) * 256,
                     oc * OC:(oc + 1) * OC].rearrange("(g p) d -> p g d", p=128)),
            dinc="ldx", waits=[("dve", outadd.get((k - 1) * 2 - 1, 0))])

    ocld[0] = c_loads(0)
    ocld[1] = c_loads(1)
    bt_load(0)
    xldQ[0] = c_xload(0)
    xldQ[1] = c_xload(1)

    for oc in range(NOC):
        par = oc % 2
        for qt in range(NQT):
            j = oc * NQT + qt
            k = oc * 4 + qt // 2
            PC = BF[j % 4]
            mm = None
            mw = [("ldw", ocld[oc]), ("dve", outadd.get(j - 4, 0))]
            for c in range(DC):
                mm = p.do("tensor", lambda e, c=c, qt=qt, par=par, PC=PC: e.matmul(
                    PC.ap()[:, 0:OC], xnT3[:, c, qt * 128:(qt + 1) * 128],
                    bstr3[par][:, c, :], start=(c == 0), stop=False),
                    inc="pe", waits=mw)
                mw = []
            mm = p.do("tensor", lambda e, qt=qt, PC=PC: e.matmul(
                PC.ap()[:, 0:OC], Ha.ap()[:, qt * 128:(qt + 1) * 128],
                BTall.ap()[:, 0:OC], start=False, stop=False), inc="pe",
                waits=[("ldw", btld[oc])])
            mm = p.do("tensor", lambda e, qt=qt, PC=PC: e.matmul(
                PC.ap()[:, 0:OC], Hb.ap()[:, qt * 128:(qt + 1) * 128],
                BTall.ap()[:, OC:2 * OC], start=False, stop=False), inc="pe")
            lora_last[oc] = mm
            for a in range(DC):
                mm = p.do("tensor", lambda e, a=a, qt=qt, par=par, PC=PC: e.matmul(
                    PC.ap()[:, 0:OC], O3[:, a, qt * 128:(qt + 1) * 128],
                    wstr3[par][:, a, :], start=False, stop=(a == DC - 1)),
                    inc="pe")
            ocmm_last[oc] = mm
            oa = p.do("vector", lambda e, j=j, qt=qt, k=k, PC=PC: e.tensor_add(
                outst[j % 2].ap()[:], PC.ap()[:, 0:OC],
                xc2[k % 2].ap()[:, (qt % 2) * OC:(qt % 2 + 1) * OC]),
                inc="dve",
                waits=[("pe", mm), ("ldx", xldQ[k]), ("st", 16 * max(0, j - 1))])
            outadd[j] = oa
            if qt % 2 == 1 and k + 2 < NOC * 4:
                xldQ[k + 2] = c_xload(k + 2)
            stores[j] = p.do("sync", lambda e, oc=oc, qt=qt, j=j: e.dma_start(
                out=out[qt * 128:(qt + 1) * 128, oc * OC:(oc + 1) * OC],
                in_=outst[j % 2].ap()[:]),
                dinc="st", waits=[("dve", oa)])
            if qt == NQT - 1 and oc + 1 < NOC:
                bt_load(oc + 1)
        if oc + 2 < NOC:
            ocld[oc + 2] = c_loads(oc + 2)

    p.do("sync", None, waits=[("st", 16 * NOC * NQT)])
    p.do("gpsimd", None, waits=[("st", 16 * NOC * NQT)])

    with nc.Block() as block:
        p.play(block)

    ctx.close()
    return nc


_NC_CACHE = None


def _get_nc():
    global _NC_CACHE
    if _NC_CACHE is None:
        _NC_CACHE = build_nc()
    return _NC_CACHE


def _host_inputs(L0_attn_x, L0_attn_wq, L0_attn_wk, L0_attn_wv, L0_attn_wo,
                 L0_ffn_base_W, L0_ffn_gate_W, L0_ffn_lora_A, L0_ffn_lora_B):
    xf = np.ascontiguousarray(np.asarray(L0_attn_x).reshape(8192, D))
    wqT = np.ascontiguousarray(np.asarray(L0_attn_wq).T)
    wkT = np.ascontiguousarray(np.asarray(L0_attn_wk).T)
    wvT = np.ascontiguousarray(np.asarray(L0_attn_wv).T)
    woT = np.ascontiguousarray(np.asarray(L0_attn_wo).T)
    baseT = np.ascontiguousarray(np.asarray(L0_ffn_base_W).T)
    gwT = np.ascontiguousarray(np.asarray(L0_ffn_gate_W).T)
    laT = np.ascontiguousarray(np.asarray(L0_ffn_lora_A).reshape(ER, D).T)
    lbT = np.ascontiguousarray(
        np.asarray(L0_ffn_lora_B).transpose(0, 2, 1).reshape(ER, D))
    ident = np.eye(128, dtype=ml_dtypes.bfloat16)

    r = np.arange(128)[:, None]
    c = np.arange(WIN)[None, :]
    band = (c - r >= 1) & (c - r <= 512)
    NEG = np.float32(-1e30)

    def mk(valid):
        return np.where(valid, np.float32(0.0), NEG)

    full5 = np.stack([mk(band)] * 5)
    m05 = np.stack([mk(band & (c >= HALO - qt * 128)) for qt in range(4)] + [mk(band)])
    masks = [np.ascontiguousarray((m05 if core == 0 else full5)
                                  .astype(ml_dtypes.bfloat16))
             for core in range(NCORES)]

    in_maps = []
    for core in range(NCORES):
        lo = core * TOWN
        if core == 0:
            xpad = np.concatenate(
                [np.zeros((HALO, D), np.float32), xf[0:TOWN]], axis=0)
        else:
            xpad = xf[lo - HALO: lo + TOWN]
        in_maps.append({
            "xpad": np.ascontiguousarray(xpad),
            "wqT": wqT, "wkT": wkT, "wvT": wvT, "woT": woT,
            "baseT": baseT, "gwT": gwT, "laT": laT, "lbT": lbT,
            "m01": masks[core], "ident": ident,
        })
    return in_maps


def kernel(L0_attn_x, L0_attn_wq, L0_attn_wk, L0_attn_wv, L0_attn_wo,
           L0_ffn_base_W, L0_ffn_gate_W, L0_ffn_lora_A, L0_ffn_lora_B):
    in_maps = _host_inputs(L0_attn_x, L0_attn_wq, L0_attn_wk, L0_attn_wv,
                           L0_attn_wo, L0_ffn_base_W, L0_ffn_gate_W,
                           L0_ffn_lora_A, L0_ffn_lora_B)
    nc = _get_nc()
    res = run_bass_kernel_spmd(nc, in_maps, core_ids=list(range(NCORES)))
    outp = np.concatenate([res.results[i]["out"] for i in range(NCORES)], axis=0)
    return outp.reshape(1, 8192, D)


# version 19
# speedup vs baseline: 1.1274x; 1.0168x over previous
import contextlib

import numpy as np
import ml_dtypes

import concourse.bass as bass
import concourse.mybir as mybir
from concourse.bass_utils import run_bass_kernel_spmd

F32 = mybir.dt.float32
BF16 = mybir.dt.bfloat16
AF = mybir.ActivationFunctionType
ALU = mybir.AluOpType
AX = mybir.AxisListType

NCORES = 8
TOWN = 1024
HALO = 512
TPAD = TOWN + HALO
D = 4096
DC = 32
NQ = 32
NKV = 8
GQA = 4
HD = 128
WIN = 640
NQT = 8
OC = 256
NOC = D // OC
NE = 28
RK = 8
ER = NE * RK
SCALE = 1.0 / float(np.sqrt(128.0))
EPS = 1e-5
ARENA = 211968

ENGS = ("sync", "gpsimd", "tensor", "vector", "scalar")


class Plan:

    def __init__(self, nc, ctx):
        self.nc = nc
        self.ops = {e: [] for e in ENGS}
        self.sems = {}
        self.cnt = {}
        self.hiw = {e: {} for e in ENGS}
        for name in ("ldx", "ldw", "st", "pe", "dve", "act", "pool"):
            self.sems[name] = ctx.enter_context(nc.semaphore(name))
            self.cnt[name] = 0

    def do(self, eng, fn, inc=None, dinc=None, waits=()):
        sname, amount = None, 0
        if inc is not None:
            sname, amount = inc, 1
        if dinc is not None:
            sname, amount = dinc, 16
        eff = []
        hw = self.hiw[eng]
        for (n, v) in waits:
            if v > hw.get(n, 0):
                hw[n] = v
                eff.append((n, v))
        if sname is not None:
            self.cnt[sname] += amount
        ret = self.cnt[sname] if sname is not None else None
        sems = self.sems

        def closure(e, fn=fn, eff=tuple(eff), sname=sname, amount=amount):
            for (n, v) in eff:
                e.wait_ge(sems[n], v)
            if fn is not None:
                inst = fn(e)
                if sname is not None:
                    inst.then_inc(sems[sname], amount)

        self.ops[eng].append(closure)
        return ret

    def barrier(self):
        snap = dict(self.cnt)
        for e in ENGS:
            self.do(e, None, waits=list(snap.items()))

    def play(self, block):
        for e in ENGS:
            lst = self.ops[e]
            if lst:
                getattr(block, e)(lambda eng, lst=lst: [f(eng) for f in lst])


DEBUG_XNT = False


def build_nc():
    nc = bass.Bass("TRN2", target_bir_lowering=False)

    xpad = nc.dram_tensor("xpad", [TPAD, D], F32, kind="ExternalInput").ap()
    wqT = nc.dram_tensor("wqT", [D, NQ * HD], F32, kind="ExternalInput").ap()
    wkT = nc.dram_tensor("wkT", [D, NKV * HD], F32, kind="ExternalInput").ap()
    wvT = nc.dram_tensor("wvT", [D, NKV * HD], F32, kind="ExternalInput").ap()
    woT = nc.dram_tensor("woT", [D, D], F32, kind="ExternalInput").ap()
    baseT = nc.dram_tensor("baseT", [D, D], F32, kind="ExternalInput").ap()
    gwT = nc.dram_tensor("gwT", [D, NE], F32, kind="ExternalInput").ap()
    laT = nc.dram_tensor("laT", [D, ER], F32, kind="ExternalInput").ap()
    lbT = nc.dram_tensor("lbT", [ER, D], F32, kind="ExternalInput").ap()
    m01 = nc.dram_tensor("m01", [5, 128, WIN], BF16, kind="ExternalInput").ap()
    ident_i = nc.dram_tensor("ident", [128, 128], BF16, kind="ExternalInput").ap()
    out = nc.dram_tensor("out", [TOWN, D], F32, kind="ExternalOutput").ap()
    if DEBUG_XNT:
        dbg_xnT = nc.dram_tensor("dbg_xnT", [128, DC * TOWN], F32, kind="ExternalOutput").ap()
        dbg_halo = nc.dram_tensor("dbg_halo", [128, DC * HALO], F32, kind="ExternalOutput").ap()

    ctx = contextlib.ExitStack()
    arena = ctx.enter_context(nc.sbuf_tensor("arena", [128, ARENA], mybir.dt.uint8))
    arena_addr = nc.lookup_mloc(arena).addr

    def at(name, shape, dtype, off):
        h = nc.virtual_sbuf_tensor(name, shape, dtype)
        nc._tensor(h.name, h.shape, h.dtype, type="SB")
        mloc = nc.lookup_mloc(h)
        mloc.allocated = not nc.target_bir_lowering
        mloc.addr = arena_addr + off
        return h

    xnT = at("xnT", [128, DC * TOWN], BF16, 0)
    ident = at("ident_sb", [128, 128], BF16, 65536)
    st = at("stats", [128, 256], F32, 65792)
    Ha = at("Ha", [112, TOWN], BF16, 66816)
    Hb = at("Hb", [112, TOWN], BF16, 68864)
    O = at("O", [128, DC * TOWN], BF16, 70912)
    haloT = at("haloT", [128, DC * HALO], BF16, 70912)
    xstg = [at("xstg0", [128, D], F32, 103680),
            at("xstg1", [128, D], F32, 120064),
            at("xstg2", [128, D], F32, 152832)]
    wvstr = [at("wvstr0", [128, DC * 256], BF16, 103680),
             at("wvstr1", [128, DC * 256], BF16, 120064)]
    K = at("K", [128, NKV * TPAD], BF16, 136448)
    V = at("V", [128, 12 * TOWN], BF16, 161024)
    sqbuf = at("sqbuf", [128, D], F32, 136448)
    xn_tm = [at("xn_tm0", [128, D], BF16, 185600),
             at("xn_tm1", [128, D], BF16, 193792)]
    wkstr = [at("wkstr0", [128, DC * 128], BF16, 185600),
             at("wkstr1", [128, DC * 128], BF16, 193792)]
    ATstr = at("ATstr", [128, DC * ER], BF16, 185600)
    gwstr = at("gwstr", [128, DC * NE], BF16, 199936)
    Hf32 = at("Hf32", [128, ER], F32, 201728)
    Hptm = at("Hptm", [128, ER], BF16, 202624)
    gbuf = at("gbuf", [128, NE], F32, 203072)
    eq1b = at("eq1b", [128, NE], F32, 203200)
    eq2b = at("eq2b", [128, NE], F32, 203328)
    mkb = at("mkb", [128, NE], F32, 203456)
    wqstr = at("wqstr", [128, DC * 128], BF16, 185600)
    Qsb = [at("Qsb0", [128, TOWN], BF16, 193792),
           at("Qsb1", [128, TOWN], BF16, 195840)]
    praw = [at("praw0", [128, WIN], BF16, 197888),
            at("praw1", [128, WIN], BF16, 199168)]
    pm = [at("pm0", [128, WIN], BF16, 200448),
          at("pm1", [128, WIN], BF16, 201728)]
    PTsb = [at("PTsb0", [128, WIN], BF16, 203008),
            at("PTsb1", [128, WIN], BF16, 204288)]
    maskb = at("maskb", [128, 5 * WIN], BF16, 205568)
    bstr = [at("bstr0", [128, DC * OC], BF16, 136448),
            at("bstr1", [128, DC * OC], BF16, 152832)]
    wstr = [at("wstr0", [128, DC * OC], BF16, 169216),
            at("wstr1", [128, DC * OC], BF16, 185600)]
    xc2 = [at("xc20", [128, 2 * OC], F32, 201984),
           at("xc21", [128, 2 * OC], F32, 204032)]
    BTall = at("BTall", [112, 2 * OC], BF16, 206080)
    outst = [at("outst0", [128, OC], F32, 208128), at("outst1", [128, OC], F32, 209152)]

    xnT3 = xnT.ap().rearrange("p (c n) -> p c n", c=DC)
    haloT3 = haloT.ap().rearrange("p (c n) -> p c n", c=DC)
    K3 = K.ap().rearrange("p (h n) -> p h n", h=NKV)
    V3 = V.ap().rearrange("p (t n) -> p t n", t=12)
    O3 = O.ap().rearrange("p (c n) -> p c n", c=DC)
    ATstr3 = ATstr.ap().rearrange("p (c n) -> p c n", c=DC)
    gwstr3 = gwstr.ap().rearrange("p (c n) -> p c n", c=DC)
    wqstr3 = wqstr.ap().rearrange("p (c n) -> p c n", c=DC)
    wvstr3 = [w.ap().rearrange("p (c n) -> p c n", c=DC) for w in wvstr]
    wkstr3 = [w.ap().rearrange("p (c n) -> p c n", c=DC) for w in wkstr]
    bstr3 = [w.ap().rearrange("p (c n) -> p c n", c=DC) for w in bstr]
    wstr3 = [w.ap().rearrange("p (c n) -> p c n", c=DC) for w in wstr]

    BF = [ctx.enter_context(nc.psum_tensor(f"psF{i}", [128, 512], F32)) for i in range(6)]
    Bb = [ctx.enter_context(nc.psum_tensor(f"psB{i}", [128, 1024], BF16)) for i in range(2)]
    BB = Bb[0]

    p = Plan(nc, ctx)

    C_EPS = 0
    C_A = lambda tt: 8 + (tt % 2) * 8
    C_M1, C_M2, C_DM, C_P1, C_P2 = 24, 25, 26, 27, 28
    C_B = lambda par: 32 + par * 8
    C_SUM = lambda par: 48 + par
    C_RCP = lambda par: 50 + par

    stap = st.ap()

    p.do("vector", lambda e: e.memset(stap[:, C_EPS:C_EPS + 1], EPS), inc="dve")
    ld_id = p.do("sync", lambda e: e.dma_start(out=ident.ap()[:], in_=ident_i[:]), dinc="ldx")

    NTT = TPAD // 128
    xn_done = {}
    var_done = {}
    trevac = {}
    ldx_t = {}
    sq_done = {}
    sum_done = {}

    def a_load(tt):
        w = [("dve", xn_done.get(tt - 3, 0))]
        ldx_t[tt] = p.do("sync",
                         lambda e, tt=tt: e.dma_start(
                             out=xstg[tt % 3].ap()[:],
                             in_=xpad[tt * 128:(tt + 1) * 128, :]),
                         dinc="ldx", waits=w)

    def a_stats1(tt):
        ca = C_A(tt)
        sq_done[tt] = p.do("scalar",
                           lambda e, tt=tt, ca=ca: e.activation(
                               sqbuf.ap()[:], xstg[tt % 3].ap()[:], AF.Square,
                               accum_out=stap[:, ca + 1:ca + 2]),
                           inc="act",
                           waits=[("ldx", ldx_t[tt]), ("dve", var_done.get(tt - 2, 0))])
        sum_done[tt] = p.do("vector",
                            lambda e, tt=tt, ca=ca: e.reduce_sum(
                                stap[:, ca:ca + 1], xstg[tt % 3].ap()[:], axis=AX.X),
                            inc="dve", waits=[("ldx", ldx_t[tt])])

    def a_stats2(tt):
        ca = C_A(tt)
        d = p.do("vector",
                 lambda e, ca=ca: e.tensor_scalar(
                     stap[:, ca + 2:ca + 3], stap[:, ca:ca + 1], -1.0 / D, None,
                     op0=ALU.mult),
                 inc="dve", waits=[("dve", sum_done[tt])])
        d = p.do("vector",
                 lambda e, ca=ca: e.tensor_scalar(
                     stap[:, ca + 3:ca + 4], stap[:, ca + 2:ca + 3],
                     stap[:, ca + 2:ca + 3], None, op0=ALU.mult),
                 inc="dve", waits=[("dve", d)])
        v_d = p.do("vector",
                   lambda e, ca=ca: e.scalar_tensor_tensor(
                       stap[:, ca + 4:ca + 5], stap[:, ca + 1:ca + 2], 1.0 / D,
                       stap[:, ca + 3:ca + 4], op0=ALU.mult, op1=ALU.subtract),
                   inc="dve", waits=[("act", sq_done[tt]), ("dve", d)])
        var_done[tt] = v_d
        d = p.do("scalar",
                 lambda e, ca=ca: e.activation(
                     stap[:, ca + 5:ca + 6], stap[:, ca + 4:ca + 5], AF.Ln,
                     bias=stap[:, C_EPS:C_EPS + 1]),
                 inc="act", waits=[("dve", v_d)])
        rs_d = p.do("scalar",
                    lambda e, ca=ca: e.activation(
                        stap[:, ca + 6:ca + 7], stap[:, ca + 5:ca + 6], AF.Exp,
                        scale=-0.5),
                    inc="act", waits=[("act", d)])
        xn_done[tt] = p.do("vector",
                           lambda e, tt=tt, ca=ca: e.tensor_scalar(
                               xn_tm[tt % 2].ap()[:], xstg[tt % 3].ap()[:],
                               stap[:, ca + 2:ca + 3], stap[:, ca + 6:ca + 7],
                               op0=ALU.add, op1=ALU.mult),
                           inc="dve",
                           waits=[("act", rs_d), ("pe", trans_done.get(tt - 2, 0))])

    trans_done = {}

    def a_trans(tt):
        for g in range(8):
            gi = tt * 8 + g
            prev_ev = trevac.get(gi - 2, 0)
            esem = "act" if g % 2 == 0 else "dve"
            tr_d = None
            for k in range(4):
                c = 4 * g + k
                tr_d = p.do("tensor",
                            lambda e, c=c, k=k, g=g, tt=tt: e.transpose(
                                Bb[g % 2].ap()[:, k * 128:(k + 1) * 128],
                                xn_tm[tt % 2].ap()[:, c * 128:(c + 1) * 128],
                                ident.ap()[:]),
                            inc="pe",
                            waits=[("dve", xn_done[tt]), (esem, prev_ev),
                                   ("ldx", ld_id)])
            if tt < 4:
                dst = haloT3[:, 4 * g:4 * g + 4, tt * 128:(tt + 1) * 128]
            else:
                dst = xnT3[:, 4 * g:4 * g + 4, (tt - 4) * 128:(tt - 3) * 128]
            if g % 2 == 0:
                ev = p.do("scalar",
                          lambda e, dst=dst, g=g: e.copy(
                              dst, Bb[g % 2].ap()[:, 0:512].rearrange("p (k n) -> p k n", k=4)),
                          inc="act", waits=[("pe", tr_d)])
            else:
                ev = p.do("vector",
                          lambda e, dst=dst, g=g: e.tensor_copy(
                              dst, Bb[g % 2].ap()[:, 0:512].rearrange("p (k n) -> p k n", k=4)),
                          inc="dve", waits=[("pe", tr_d)])
            trevac[gi] = ev
        trans_done[tt] = p.cnt["pe"]

    a_load(0)
    a_load(1)
    for tt in range(NTT + 2):
        if tt < NTT:
            a_stats1(tt)
        if 0 <= tt - 1 < NTT:
            a_stats2(tt - 1)
        if tt + 2 < NTT:
            a_load(tt + 2)
        if 0 <= tt - 2 < NTT:
            a_trans(tt - 2)

    p.barrier()

    if DEBUG_XNT:
        dzc = {}
        for half in range(16):
            dzc[half] = p.do("vector", lambda e, half=half: e.tensor_copy(
                xstg[0].ap()[:, 0:2048].rearrange("p (a b) -> p a b", a=1),
                xnT.ap()[:, half * 2048:(half + 1) * 2048].rearrange("p (a b) -> p a b", a=1)),
                inc="dve")
            p.do("sync", lambda e, half=half: e.dma_start(
                out=dbg_xnT[:, half * 2048:(half + 1) * 2048], in_=xstg[0].ap()[:, 0:2048]),
                dinc="st", waits=[("dve", dzc[half])])
            p.barrier()
        for half in range(8):
            dzc[half] = p.do("vector", lambda e, half=half: e.tensor_copy(
                xstg[0].ap()[:, 0:2048],
                haloT.ap()[:, half * 2048:(half + 1) * 2048]),
                inc="dve")
            p.do("sync", lambda e, half=half: e.dma_start(
                out=dbg_halo[:, half * 2048:(half + 1) * 2048], in_=xstg[0].ap()[:, 0:2048]),
                dinc="st", waits=[("dve", dzc[half])])
            p.barrier()

    def slab_rhs(s, width, c):
        if s == 0:
            return haloT3[:, c, 0:width]
        return xnT3[:, c, (s - 1) * 512:(s - 1) * 512 + width]

    kld = {}
    kmm = {}
    kevac = {}

    def k_load(kd):
        return p.do("gpsimd",
                    lambda e, kd=kd: e.dma_start(
                        out=wkstr3[kd % 2],
                        in_=wkT.rearrange("(c p) n -> p c n", p=128)[:, :, kd * 128:(kd + 1) * 128]),
                    dinc="ldw", waits=[("pe", kmm.get(kd - 2, 0))])

    kld[0] = k_load(0)
    kld[1] = k_load(1)
    for kd in range(NKV):
        for s in range(3):
            ps = BF[(kd * 3 + s) % 2]
            mm = None
            for c in range(DC):
                mm = p.do("tensor",
                          lambda e, kd=kd, s=s, c=c, ps=ps: e.matmul(
                              ps.ap()[:], wkstr3[kd % 2][:, c, :], slab_rhs(s, 512, c),
                              start=(c == 0), stop=(c == DC - 1)),
                          inc="pe",
                          waits=[("ldw", kld[kd]),
                                 ("dve", kevac.get((kd * 3 + s) - 2, 0))])
            kmm[kd] = mm
            kevac[kd * 3 + s] = p.do(
                "vector",
                lambda e, kd=kd, s=s, ps=ps: e.tensor_copy(
                    K3[:, kd, s * 512:(s + 1) * 512], ps.ap()[:]),
                inc="dve", waits=[("pe", mm)])
        if kd + 2 < NKV:
            kld[kd + 2] = k_load(kd + 2)

    at_ld = p.do("gpsimd",
                 lambda e: e.dma_start(out=ATstr3,
                                       in_=laT.rearrange("(c p) n -> p c n", p=128)),
                 dinc="ldw", waits=[("pe", kmm[NKV - 1])])
    gw_ld = p.do("gpsimd",
                 lambda e: e.dma_start(out=gwstr3,
                                       in_=gwT.rearrange("(c p) n -> p c n", p=128)),
                 dinc="ldw")

    vld = {}
    vmm = {}
    vevac = {}

    def v_load(q):
        return p.do("gpsimd",
                    lambda e, q=q: e.dma_start(
                        out=wvstr3[q % 2],
                        in_=wvT.rearrange("(c p) n -> p c n", p=128)[:, :, q * 256:(q + 1) * 256]),
                    dinc="ldw", waits=[("pe", vmm.get(q - 2, 0))])

    vld[0] = v_load(0)
    vld[1] = v_load(1)
    vi = 0
    for q in range(4):
        last_mm_q = None
        for tt in range(12):
            ps = BF[2 + vi % 2]
            if tt < 4:
                lhsT = lambda c, tt=tt: haloT3[:, c, tt * 128:(tt + 1) * 128]
            else:
                lhsT = lambda c, tt=tt: xnT3[:, c, (tt - 4) * 128:(tt - 3) * 128]
            mm = None
            for c in range(DC):
                mm = p.do("tensor",
                          lambda e, c=c, ps=ps, lhsT=lhsT, q=q: e.matmul(
                              ps.ap()[:, 0:256], lhsT(c), wvstr3[q % 2][:, c, :],
                              start=(c == 0), stop=(c == DC - 1)),
                          inc="pe",
                          waits=[("ldw", vld[q]), ("dve", vevac.get(vi - 2, 0))])
            last_mm_q = mm
            vevac[vi] = p.do("vector",
                             lambda e, tt=tt, q=q, ps=ps: e.tensor_copy(
                                 V3[:, tt, q * 256:(q + 1) * 256], ps.ap()[:, 0:256]),
                             inc="dve", waits=[("pe", mm)])
            vi += 1
        vmm[q] = last_mm_q
        if q + 2 < 4:
            vld[q + 2] = v_load(q + 2)

    a2_dve = p.cnt["dve"]

    mask_ld = p.do("sync",
                   lambda e: e.dma_start(
                       out=maskb.ap()[:].rearrange("p (t w) -> p t w", t=5),
                       in_=m01.rearrange("t p w -> p t w")),
                   dinc="ldx")
    lg_read = a2_dve
    hevac_prev = 0
    hm_bank = a2_dve
    htev = {}
    for qt in range(NQT):
        LG = BF[0]
        HM = BF[1]
        mm = None
        for c in range(DC):
            mm = p.do("tensor",
                      lambda e, c=c, qt=qt, LG=LG: e.matmul(
                          LG.ap()[:, 0:NE], xnT3[:, c, qt * 128:(qt + 1) * 128],
                          gwstr3[:, c, :], start=(c == 0), stop=(c == DC - 1)),
                      inc="pe", waits=[("ldw", gw_ld), ("dve", lg_read)])
        m1_d = p.do("vector", lambda e, LG=LG: e.reduce_max(
            stap[:, C_M1:C_M1 + 1], LG.ap()[:, 0:NE], axis=AX.X),
            inc="dve", waits=[("pe", mm)])
        e1_d = p.do("vector", lambda e, LG=LG: e.tensor_scalar(
            eq1b.ap()[:], LG.ap()[:, 0:NE], stap[:, C_M1:C_M1 + 1], None,
            op0=ALU.is_equal), inc="dve", waits=[("dve", m1_d)])
        mk_d = p.do("vector", lambda e, LG=LG: e.scalar_tensor_tensor(
            mkb.ap()[:], eq1b.ap()[:], -1e30, LG.ap()[:, 0:NE],
            op0=ALU.mult, op1=ALU.add), inc="dve", waits=[("dve", e1_d)])
        lg_read = mk_d
        m2_d = p.do("vector", lambda e: e.reduce_max(
            stap[:, C_M2:C_M2 + 1], mkb.ap()[:], axis=AX.X),
            inc="dve", waits=[("dve", mk_d)])
        e2_d = p.do("vector", lambda e: e.tensor_scalar(
            eq2b.ap()[:], mkb.ap()[:], stap[:, C_M2:C_M2 + 1], None,
            op0=ALU.is_equal), inc="dve", waits=[("dve", m2_d)])
        dm_d = p.do("vector", lambda e: e.tensor_tensor(
            stap[:, C_DM:C_DM + 1], stap[:, C_M2:C_M2 + 1],
            stap[:, C_M1:C_M1 + 1], op=ALU.subtract),
            inc="dve", waits=[("dve", e2_d)])
        p1_d = p.do("scalar", lambda e: e.activation(
            stap[:, C_P1:C_P1 + 1], stap[:, C_DM:C_DM + 1], AF.Sigmoid, scale=-1.0),
            inc="act", waits=[("dve", dm_d)])
        p2_d = p.do("scalar", lambda e: e.activation(
            stap[:, C_P2:C_P2 + 1], stap[:, C_DM:C_DM + 1], AF.Sigmoid),
            inc="act", waits=[("act", p1_d)])
        g1_d = p.do("vector", lambda e: e.tensor_scalar(
            gbuf.ap()[:], eq1b.ap()[:], stap[:, C_P1:C_P1 + 1], 2.0,
            op0=ALU.mult, op1=ALU.mult), inc="dve", waits=[("act", p2_d)])
        g2_d = p.do("vector", lambda e: e.tensor_scalar(
            eq2b.ap()[:], eq2b.ap()[:], stap[:, C_P2:C_P2 + 1], 2.0,
            op0=ALU.mult, op1=ALU.mult), inc="dve", waits=[("dve", g1_d)])
        g_d = p.do("vector", lambda e: e.tensor_add(
            gbuf.ap()[:], gbuf.ap()[:], eq2b.ap()[:]),
            inc="dve", waits=[("dve", g2_d)])

        mm2 = None
        for c in range(DC):
            mm2 = p.do("tensor",
                       lambda e, c=c, qt=qt, HM=HM: e.matmul(
                           HM.ap()[:, 0:ER], xnT3[:, c, qt * 128:(qt + 1) * 128],
                           ATstr3[:, c, :], start=(c == 0), stop=(c == DC - 1)),
                       inc="pe", waits=[("ldw", at_ld), ("act", hevac_prev),
                                        ("dve", hm_bank)])
        hev = p.do("scalar", lambda e, HM=HM: e.copy(Hf32.ap()[:], HM.ap()[:, 0:ER]),
                   inc="act", waits=[("pe", mm2)])
        hevac_prev = hev
        hp_d = None
        for ex in range(NE):
            hp_d = p.do("vector",
                        lambda e, ex=ex: e.tensor_scalar(
                            Hptm.ap()[:, ex * RK:(ex + 1) * RK],
                            Hf32.ap()[:, ex * RK:(ex + 1) * RK],
                            gbuf.ap()[:, ex:ex + 1], None, op0=ALU.mult),
                        inc="dve", waits=[("act", hev), ("dve", g_d)])
        t1 = p.do("tensor", lambda e: e.transpose(
            BB.ap()[0:112, 0:128], Hptm.ap()[:, 0:112], ident.ap()[:]),
            inc="pe", waits=[("dve", hp_d), ("act", htev.get(qt - 1, 0))])
        t2 = p.do("tensor", lambda e: e.transpose(
            BB.ap()[0:112, 128:256], Hptm.ap()[:, 112:224], ident.ap()[:]),
            inc="pe", waits=[("dve", hp_d)])
        ev1 = p.do("scalar", lambda e, qt=qt: e.copy(
            Ha.ap()[:, qt * 128:(qt + 1) * 128], BB.ap()[0:112, 0:128]),
            inc="act", waits=[("pe", t2)])
        ev2 = p.do("scalar", lambda e, qt=qt: e.copy(
            Hb.ap()[:, qt * 128:(qt + 1) * 128], BB.ap()[0:112, 128:256]),
            inc="act", waits=[("pe", t2)])
        htev[qt] = ev2

    p.barrier()

    maskb3 = maskb.ap().rearrange("p (t w) -> p t w", t=5)

    NT = NQ * NQT
    wqld = {}
    qmm_done = {}
    qevac = {}
    smm = {}
    mread = {}
    edone = {}
    ndone = {}
    tsn = {}
    tdone = {}
    pdone = {}
    vdone = {}
    odone = {}

    def st_qload(h):
        wqld[h] = p.do("gpsimd",
                       lambda e, h=h: e.dma_start(
                           out=wqstr3,
                           in_=wqT.rearrange("(c p) n -> p c n", p=128)[:, :, h * 128:(h + 1) * 128]),
                       dinc="ldw", waits=[("pe", qmm_done.get(h - 1, 0))])

    def st_qhalf(h, half):
        qmm = None
        for c in range(DC):
            qmm = p.do("tensor",
                       lambda e, c=c, half=half: e.matmul(
                           BF[0].ap()[:], wqstr3[:, c, :],
                           xnT3[:, c, half * 512:(half + 1) * 512],
                           start=(c == 0), stop=(c == DC - 1)),
                       inc="pe",
                       waits=[("ldw", wqld[h]),
                              ("act", qevac.get((h, half - 1), qevac.get((h - 1, 1), 0)))])
        qmm_done[h] = qmm
        qevac[(h, half)] = p.do("scalar",
                                lambda e, h=h, half=half: e.copy(
                                    Qsb[h % 2].ap()[:, half * 512:(half + 1) * 512],
                                    BF[0].ap()[:]),
                                inc="act",
                                waits=[("pe", max(qmm, smm.get((h - 2) * NQT + NQT - 1, 0)))])

    def st_S(i):
        h, qt = divmod(i, NQT)
        kvh = h // GQA
        par = i % 2
        SA, SB = BF[1 + par], BF[3 + par]
        mt = qt if qt < 4 else 4
        p.do("tensor",
             lambda e, qt=qt, kvh=kvh, SA=SA, h=h: e.matmul(
                 SA.ap()[:], Qsb[h % 2].ap()[:, qt * 128:(qt + 1) * 128],
                 K3[:, kvh, qt * 128:qt * 128 + 512], start=True, stop=False),
             inc="pe",
             waits=[("act", max(qevac[(h, 1)], edone.get(i - 2, 0))),
                    ("dve", mread.get(i - 2, 0)),
                    ("ldx", mask_ld)])
        p.do("tensor",
             lambda e, mt=mt, SA=SA: e.matmul(
                 SA.ap()[:], ident.ap()[:], maskb3[:, mt, 0:512],
                 start=False, stop=True), inc="pe")
        p.do("tensor",
             lambda e, qt=qt, kvh=kvh, SB=SB, h=h: e.matmul(
                 SB.ap()[:, 0:128], Qsb[h % 2].ap()[:, qt * 128:(qt + 1) * 128],
                 K3[:, kvh, qt * 128 + 512:qt * 128 + WIN], start=True, stop=False),
             inc="pe")
        smm[i] = p.do("tensor",
                      lambda e, mt=mt, SB=SB: e.matmul(
                          SB.ap()[:, 0:128], ident.ap()[:], maskb3[:, mt, 512:WIN],
                          start=False, stop=True), inc="pe")

    mnms = {}

    def st_M(i):
        par = i % 2
        SA, SB = BF[1 + par], BF[3 + par]
        cb = C_B(par)
        d = p.do("vector",
                 lambda e, SA=SA, cb=cb: e.reduce_max(
                     stap[:, cb:cb + 1], SA.ap()[:], axis=AX.X, negate=True),
                 inc="dve",
                 waits=[("pe", smm[i]), ("act", edone.get(i - 2, 0))])
        mread[i] = d
        mnms[i] = p.do("vector",
                       lambda e, cb=cb: e.tensor_scalar(
                           stap[:, cb + 2:cb + 3], stap[:, cb:cb + 1], 113.0,
                           SCALE, op0=ALU.min, op1=ALU.mult),
                       inc="dve", waits=[("dve", d)])

    def st_E(i):
        par = i % 2
        SA, SB = BF[1 + par], BF[3 + par]
        cb = C_B(par)
        p.do("scalar",
             lambda e, par=par, SA=SA, cb=cb: e.activation(
                 praw[par].ap()[:, 0:512], SA.ap()[:], AF.Exp,
                 bias=stap[:, cb + 2:cb + 3], scale=SCALE,
                 accum_out=stap[:, cb + 3:cb + 4]),
             inc="act",
             waits=[("dve", max(mnms[i], tsn.get(i - 2, 0)))])
        edone[i] = p.do("scalar",
                        lambda e, par=par, SB=SB, cb=cb: e.activation(
                            praw[par].ap()[:, 512:WIN], SB.ap()[:, 0:128], AF.Exp,
                            bias=stap[:, cb + 2:cb + 3], scale=SCALE,
                            accum_out=stap[:, cb + 4:cb + 5]),
                        inc="act", waits=[])

    def st_N(i):
        par = i % 2
        cb = C_B(par)
        d = p.do("vector",
                 lambda e, cb=cb: e.tensor_add(
                     stap[:, cb + 5:cb + 6], stap[:, cb + 3:cb + 4],
                     stap[:, cb + 4:cb + 5]),
                 inc="dve", waits=[("act", edone[i])])
        d = p.do("vector",
                 lambda e, cb=cb: e.reciprocal(
                     stap[:, cb + 6:cb + 7], stap[:, cb + 5:cb + 6]),
                 inc="dve", waits=[("dve", d)])
        ndone[i] = d
        tsn[i] = p.do("vector",
                      lambda e, par=par, cb=cb: e.tensor_scalar(
                          pm[par].ap()[:], praw[par].ap()[:],
                          stap[:, cb + 6:cb + 7], None, op0=ALU.mult),
                      inc="dve",
                      waits=[("dve", d), ("pe", tdone.get(i - 2, 0))])

    def st_T(i):
        par = i % 2
        d = None
        for c in range(5):
            d = p.do("tensor",
                     lambda e, c=c, par=par: e.transpose(
                         Bb[par].ap()[:, c * 128:(c + 1) * 128],
                         pm[par].ap()[:, c * 128:(c + 1) * 128], ident.ap()[:]),
                     inc="pe",
                     waits=[("dve", tsn[i]), ("act", pdone.get(i - 2, 0))])
        tdone[i] = d

    def st_P(i):
        par = i % 2
        pdone[i] = p.do("scalar",
                        lambda e, par=par: e.copy(PTsb[par].ap()[:],
                                                  Bb[par].ap()[:, 0:WIN]),
                        inc="act",
                        waits=[("pe", max(tdone[i], vdone.get(i - 2, 0)))])

    def st_V(i):
        h, qt = divmod(i, NQT)
        kvh = h // GQA
        par = i % 2
        d = None
        for c in range(5):
            d = p.do("tensor",
                     lambda e, c=c, qt=qt, kvh=kvh, par=par: e.matmul(
                         BF[5].ap()[:, 0:128],
                         V3[:, qt + c, kvh * 128:(kvh + 1) * 128],
                         PTsb[par].ap()[:, c * 128:(c + 1) * 128],
                         start=(c == 0), stop=(c == 4)),
                     inc="pe",
                     waits=[("act", max(pdone[i], odone.get(i - 1, 0)))])
        vdone[i] = d

    def st_O(i):
        h, qt = divmod(i, NQT)
        odone[i] = p.do("scalar",
                        lambda e, h=h, qt=qt: e.copy(
                            O3[:, h, qt * 128:(qt + 1) * 128], BF[5].ap()[:, 0:128]),
                        inc="act", waits=[("pe", vdone[i])])

    st_qload(0)
    st_qhalf(0, 0)
    st_qhalf(0, 1)

    for i in range(NT + 3):
        if i < NT:
            r = i % NQT
            h = i // NQT
            if r == 2 and h + 1 < NQ:
                st_qload(h + 1)
            if r == 6 and h + 1 < NQ:
                st_qhalf(h + 1, 0)
            if r == 7 and h + 1 < NQ:
                st_qhalf(h + 1, 1)
            st_S(i)
        if 0 <= i - 1 < NT:
            st_M(i - 1)
            st_E(i - 1)
        if 0 <= i - 2 < NT:
            st_N(i - 2)
            st_T(i - 2)
            st_P(i - 2)
        if 0 <= i - 3 < NT:
            st_V(i - 3)
            st_O(i - 3)

    p.barrier()

    ocld = {}
    ocmm_last = {}
    lora_last = {}
    outadd = {}
    xldQ = {}
    stores = {}
    NJ = NOC * NQT

    def c_loads(oc):
        par = oc % 2
        w = [("pe", ocmm_last.get(oc - 2, 0))]
        p.do("gpsimd", lambda e, oc=oc, par=par: e.dma_start(
            out=bstr3[par],
            in_=baseT.rearrange("(c p) n -> p c n", p=128)[:, :, oc * OC:(oc + 1) * OC]),
            dinc="ldw", waits=w)
        return p.do("gpsimd", lambda e, oc=oc, par=par: e.dma_start(
            out=wstr3[par],
            in_=woT.rearrange("(c p) n -> p c n", p=128)[:, :, oc * OC:(oc + 1) * OC]),
            dinc="ldw", waits=w)

    btld = {}

    def bt_load(oc):
        btld[oc] = p.do("gpsimd", lambda e, oc=oc: e.dma_start(
            out=BTall.ap()[:].rearrange("p (g o) -> p g o", g=2),
            in_=lbT.rearrange("(g p) o -> p g o", p=112)[:, :, oc * OC:(oc + 1) * OC]),
            dinc="ldw", waits=[("pe", lora_last.get(oc - 1, 0))])

    def c_xload(k):
        oc, q2 = divmod(k, 4)
        return p.do("sync", lambda e, oc=oc, q2=q2, k=k: e.dma_start(
            out=xc2[k % 2].ap()[:].rearrange("p (g d) -> p g d", g=2),
            in_=xpad[HALO + q2 * 256:HALO + (q2 + 1) * 256,
                     oc * OC:(oc + 1) * OC].rearrange("(g p) d -> p g d", p=128)),
            dinc="ldx", waits=[("dve", outadd.get((k - 1) * 2 - 1, 0))])

    ocld[0] = c_loads(0)
    ocld[1] = c_loads(1)
    bt_load(0)
    xldQ[0] = c_xload(0)
    xldQ[1] = c_xload(1)

    for oc in range(NOC):
        par = oc % 2
        for qt in range(NQT):
            j = oc * NQT + qt
            k = oc * 4 + qt // 2
            PC = BF[j % 4]
            mm = None
            mw = [("ldw", ocld[oc]), ("dve", outadd.get(j - 4, 0))]
            for c in range(DC):
                mm = p.do("tensor", lambda e, c=c, qt=qt, par=par, PC=PC: e.matmul(
                    PC.ap()[:, 0:OC], xnT3[:, c, qt * 128:(qt + 1) * 128],
                    bstr3[par][:, c, :], start=(c == 0), stop=False),
                    inc="pe", waits=mw)
                mw = []
            mm = p.do("tensor", lambda e, qt=qt, PC=PC: e.matmul(
                PC.ap()[:, 0:OC], Ha.ap()[:, qt * 128:(qt + 1) * 128],
                BTall.ap()[:, 0:OC], start=False, stop=False), inc="pe",
                waits=[("ldw", btld[oc])])
            mm = p.do("tensor", lambda e, qt=qt, PC=PC: e.matmul(
                PC.ap()[:, 0:OC], Hb.ap()[:, qt * 128:(qt + 1) * 128],
                BTall.ap()[:, OC:2 * OC], start=False, stop=False), inc="pe")
            lora_last[oc] = mm
            for a in range(DC):
                mm = p.do("tensor", lambda e, a=a, qt=qt, par=par, PC=PC: e.matmul(
                    PC.ap()[:, 0:OC], O3[:, a, qt * 128:(qt + 1) * 128],
                    wstr3[par][:, a, :], start=False, stop=(a == DC - 1)),
                    inc="pe")
            ocmm_last[oc] = mm
            oa = p.do("vector", lambda e, j=j, qt=qt, k=k, PC=PC: e.tensor_add(
                outst[j % 2].ap()[:], PC.ap()[:, 0:OC],
                xc2[k % 2].ap()[:, (qt % 2) * OC:(qt % 2 + 1) * OC]),
                inc="dve",
                waits=[("pe", mm), ("ldx", xldQ[k]), ("st", 16 * max(0, j - 1))])
            outadd[j] = oa
            if qt % 2 == 1 and k + 2 < NOC * 4:
                xldQ[k + 2] = c_xload(k + 2)
            stores[j] = p.do("sync", lambda e, oc=oc, qt=qt, j=j: e.dma_start(
                out=out[qt * 128:(qt + 1) * 128, oc * OC:(oc + 1) * OC],
                in_=outst[j % 2].ap()[:]),
                dinc="st", waits=[("dve", oa)])
            if qt == NQT - 1 and oc + 1 < NOC:
                bt_load(oc + 1)
        if oc + 2 < NOC:
            ocld[oc + 2] = c_loads(oc + 2)

    p.do("sync", None, waits=[("st", 16 * NOC * NQT)])
    p.do("gpsimd", None, waits=[("st", 16 * NOC * NQT)])

    with nc.Block() as block:
        p.play(block)

    ctx.close()
    return nc


_NC_CACHE = None


def _get_nc():
    global _NC_CACHE
    if _NC_CACHE is None:
        _NC_CACHE = build_nc()
    return _NC_CACHE


def _host_inputs(L0_attn_x, L0_attn_wq, L0_attn_wk, L0_attn_wv, L0_attn_wo,
                 L0_ffn_base_W, L0_ffn_gate_W, L0_ffn_lora_A, L0_ffn_lora_B):
    xf = np.ascontiguousarray(np.asarray(L0_attn_x).reshape(8192, D))
    wqT = np.ascontiguousarray(np.asarray(L0_attn_wq).T)
    wkT = np.ascontiguousarray(np.asarray(L0_attn_wk).T)
    wvT = np.ascontiguousarray(np.asarray(L0_attn_wv).T)
    woT = np.ascontiguousarray(np.asarray(L0_attn_wo).T)
    baseT = np.ascontiguousarray(np.asarray(L0_ffn_base_W).T)
    gwT = np.ascontiguousarray(np.asarray(L0_ffn_gate_W).T)
    laT = np.ascontiguousarray(np.asarray(L0_ffn_lora_A).reshape(ER, D).T)
    lbT = np.ascontiguousarray(
        np.asarray(L0_ffn_lora_B).transpose(0, 2, 1).reshape(ER, D))
    ident = np.eye(128, dtype=ml_dtypes.bfloat16)

    r = np.arange(128)[:, None]
    c = np.arange(WIN)[None, :]
    band = (c - r >= 1) & (c - r <= 512)
    NEG = np.float32(-1e30)

    def mk(valid):
        return np.where(valid, np.float32(0.0), NEG)

    full5 = np.stack([mk(band)] * 5)
    m05 = np.stack([mk(band & (c >= HALO - qt * 128)) for qt in range(4)] + [mk(band)])
    masks = [np.ascontiguousarray((m05 if core == 0 else full5)
                                  .astype(ml_dtypes.bfloat16))
             for core in range(NCORES)]

    in_maps = []
    for core in range(NCORES):
        lo = core * TOWN
        if core == 0:
            xpad = np.concatenate(
                [np.zeros((HALO, D), np.float32), xf[0:TOWN]], axis=0)
        else:
            xpad = xf[lo - HALO: lo + TOWN]
        in_maps.append({
            "xpad": np.ascontiguousarray(xpad),
            "wqT": wqT, "wkT": wkT, "wvT": wvT, "woT": woT,
            "baseT": baseT, "gwT": gwT, "laT": laT, "lbT": lbT,
            "m01": masks[core], "ident": ident,
        })
    return in_maps


def kernel(L0_attn_x, L0_attn_wq, L0_attn_wk, L0_attn_wv, L0_attn_wo,
           L0_ffn_base_W, L0_ffn_gate_W, L0_ffn_lora_A, L0_ffn_lora_B):
    in_maps = _host_inputs(L0_attn_x, L0_attn_wq, L0_attn_wk, L0_attn_wv,
                           L0_attn_wo, L0_ffn_base_W, L0_ffn_gate_W,
                           L0_ffn_lora_A, L0_ffn_lora_B)
    nc = _get_nc()
    res = run_bass_kernel_spmd(nc, in_maps, core_ids=list(range(NCORES)))
    outp = np.concatenate([res.results[i]["out"] for i in range(NCORES)], axis=0)
    return outp.reshape(1, 8192, D)
